# version 21
# baseline (speedup 1.0000x reference)
import numpy as np
import concourse.bass as bass
import concourse.mybir as mybir
from concourse.alu_op_type import AluOpType

F32 = mybir.dt.float32
AF = mybir.ActivationFunctionType
AX = mybir.AxisListType.X

C = 1024
KC = C // 128
I = 2816
IC = I // 128


class Sem:
    def __init__(self, handle, engine, dma=False):
        self.h = handle
        self.engine = engine
        self.n = 0
        self.step = 16 if dma else 1


class Buf:
    __slots__ = ("name", "w", "r")

    def __init__(self, name, seed=None):
        self.name = name
        self.w = dict(seed.w) if seed else {}
        self.r = dict(seed.r) if seed else {}
        if seed:
            for s, v in seed.r.items():
                if self.w.get(s, -1) < v:
                    self.w[s] = v

    @staticmethod
    def merge(d, sem, val):
        if val is not None and d.get(sem, -1) < val:
            d[sem] = val


class Arena:
    def __init__(self, name):
        self.name = name
        self.tail = Buf(name + "_tail")
        self.live = []

    def role(self, name):
        b = Buf(name, seed=self.tail)
        self.live.append(b)
        return b

    def new_era(self):
        for b in self.live:
            for s, v in b.w.items():
                Buf.merge(self.tail.w, s, v)
            for s, v in b.r.items():
                Buf.merge(self.tail.r, s, v)
        self.live = []


class Builder:
    def __init__(self):
        self.steps = []
        self.waited = {}

    def op(self, engine, fn, reads=(), writes=(), inc=None, extra=()):
        waits = {}
        def note(s, v):
            if s.step == 16:
                v = s.n
            if v and waits.get(s, -1) < v:
                waits[s] = v
        for b in reads:
            for s, v in b.w.items():
                note(s, v)
        for b in writes:
            for d in (b.w, b.r):
                for s, v in d.items():
                    note(s, v)
        for s, v in extra:
            note(s, v)
        val = None
        if inc is not None:
            inc.n += inc.step
            val = inc.n
        wl = []
        for s, v in sorted(waits.items(), key=lambda x: id(x[0])):
            key = (engine, s)
            if self.waited.get(key, -1) >= v:
                continue
            self.waited[key] = v
            wl.append((s, v))

        def step(eng, fn=fn, wl=wl, inc=inc):
            for s, v in wl:
                eng.wait_ge(s.h, v)
            last = fn(eng)
            if inc is not None:
                assert last is not None, "op with inc= must return an instruction"
                last.then_inc(inc.h, inc.step)

        self.steps.append((engine, step))
        if inc is not None:
            for b in reads:
                Buf.merge(b.r, inc, val)
            for b in writes:
                Buf.merge(b.w, inc, val)
        return (inc, val)


def build(nc, T=2048, BLOC=2, dbg=(), stage="full"):
    stage0 = stage
    stage = {"p3a": "full", "p3b": "full"}.get(stage, stage)
    dbg = set(dbg)
    TG = T // 128
    TBA = min(512, T)
    NTB = T // TBA
    SL = max(128, min(512, T // 4))
    NSLAB = T // SL
    FTB = min(256, SL)
    NFTB = SL // FTB
    NG_TB = FTB // 128
    TS = SL

    x_in = nc.dram_tensor("x", [BLOC * T, C], F32, kind="ExternalInput").ap()
    rkvw = nc.dram_tensor("rkvw_t", [C, 6 * C], F32, kind="ExternalInput").ap()
    outw = nc.dram_tensor("outw_t", [2 * C, C], F32, kind="ExternalInput").ap()
    wg_d = nc.dram_tensor("wg", [I, C], F32, kind="ExternalInput").ap()
    wu_d = nc.dram_tensor("wu", [I, C], F32, kind="ExternalInput").ap()
    wd_d = nc.dram_tensor("wd", [C, I], F32, kind="ExternalInput").ap()
    tdec = nc.dram_tensor("tdec", [2, C], F32, kind="ExternalInput").ap()
    tfir = nc.dram_tensor("tfir", [2, C], F32, kind="ExternalInput").ap()
    iden_d = nc.dram_tensor("iden", [128, 128], F32, kind="ExternalInput").ap()
    out_d = nc.dram_tensor("out", [BLOC * T, C], F32, kind="ExternalOutput").ap()
    wgq_c = nc.dram_tensor("wgq_cache", [IC, 128, KC * 128], F32, kind="Internal").ap()
    wgq_r = nc.dram_tensor("wgq_raw", [IC, 128, C], F32, kind="Internal").ap()
    wuq_r = nc.dram_tensor("wuq_raw", [IC, 128, C], F32, kind="Internal").ap()
    wdq_r = nc.dram_tensor("wdq_raw", [KC, 128, I], F32, kind="Internal").ap()
    wuq_c = nc.dram_tensor("wuq_cache", [IC, 128, KC * 128], F32, kind="Internal").ap()
    wdq_c = nc.dram_tensor("wdq_cache", [KC, 128, IC * 128], F32, kind="Internal").ap()
    s_acc = nc.dram_tensor("s_acc", [C, T], F32, kind="Internal").ap()
    s_x2 = nc.dram_tensor("s_x2", [T, C], F32, kind="Internal").ap()
    s_xq = nc.dram_tensor("s_xq", [C, T], F32, kind="Internal").ap()

    dbg_aps = {}
    def mkdbg(name, shape):
        if name in dbg:
            dbg_aps[name] = nc.dram_tensor("dbg_" + name, shape, F32,
                                           kind="ExternalOutput").ap()
        return dbg_aps.get(name)
    d_lnT = mkdbg("lnT", [C, T])
    d_of = mkdbg("of", [C, T])
    d_ob = mkdbg("ob", [C, T])
    d_x2 = mkdbg("x2", [T, C])
    d_xqT = mkdbg("xqT", [C, SL])
    d_wgq = mkdbg("wgq", [I, C])
    d_wdq = mkdbg("wdq", [I, C])
    d_hq = mkdbg("hq", [FTB, I])

    used_kb = [0.0]
    def sb(name, cols):
        used_kb[0] += cols * 4 / 1024
        return nc.alloc_sbuf_tensor(name, [128, cols], F32).ap()

    szA = max(KC * T, 2 * KC * SL + 2 * NG_TB * C + 4 * FTB + 2 * C)
    szB = max(KC * T, 5 * I + C)
    szC = max(4 * T + 2, IC * FTB + IC * 128, 7 * C + 16)
    A = sb("arA", szA)
    Bb = sb("arB", szB)
    Cc = sb("arC", szC)
    iden = sb("s_iden", 128)
    ewb = sb("s_ew", 2 * KC)
    eub = sb("s_eu", 2 * KC)
    tscr = sb("s_tscr", 2 * KC)
    xg = sb("s_xg", C)
    lng = sb("s_lng", C)
    st = sb("s_st", 16)
    wmix = sb("s_wmix", 3 * KC * 128)
    accs = sb("s_accs", 2 * 512)
    attg = sb("s_attg", 2 * 128)
    wgqT = sb("s_wgqT", 2 * KC * 128)
    wuqT = sb("s_wuqT", 2 * KC * 128)
    q_bm = sb("s_qbm", I // 16)
    q_sc = sb("s_qsc", I // 16)
    q_si = sb("s_qsi", I // 16)
    q_bm2 = sb("s_qbm2", I // 16)
    q_sc2 = sb("s_qsc2", I // 16)
    q_si2 = sb("s_qsi2", I // 16)
    epsb = sb("s_eps", 1)
    cmid = sb("s_cmid", 8)
    cmidn = sb("s_cmidn", 8)
    assert used_kb[0] < 214, f"SBUF {used_kb[0]:.1f}KB"

    lnT = A[:, 0:KC * T]
    xqTs = [A[:, k * KC * SL:(k + 1) * KC * SL] for k in range(2)]
    _o = 2 * KC * SL
    x2g = A[:, _o: _o + NG_TB * C]
    og = A[:, _o + NG_TB * C: _o + 2 * NG_TB * C]
    hfm2 = [A[:, _o + 2 * NG_TB * C + k * FTB: _o + 2 * NG_TB * C + (k + 1) * FTB] for k in range(2)]
    dsb2 = [A[:, _o + 2 * NG_TB * C + (2 + k) * FTB: _o + 2 * NG_TB * C + (3 + k) * FTB] for k in range(2)]
    _p3_end = _o + 2 * NG_TB * C + 4 * FTB
    wraw = A[:, _p3_end:_p3_end + 2 * C]
    assert _p3_end + 2 * C <= szA

    QW = 1408
    ofb = Bb[:, 0:KC * T]
    q_y = Bb[:, 0:QW]
    q_n = Bb[:, QW:2 * QW]
    q_dq = Bb[:, 2 * QW:3 * QW]
    htok = Bb[:, 3 * QW:3 * QW + 2 * NG_TB * I]
    assert 3 * QW + 2 * NG_TB * I <= szB
    qc_y = [Cc[:, 0:C], Cc[:, 3 * C:4 * C]]
    qc_n = [Cc[:, C:2 * C], Cc[:, 4 * C:5 * C]]
    qc_dq = [Cc[:, 2 * C:3 * C], Cc[:, 5 * C:6 * C]]
    sqd = [Cc[:, 2 * C:3 * C], Cc[:, 5 * C:6 * C]]
    lng2 = [lng, Cc[:, 6 * C:7 * C]]
    st2 = [st, Cc[:, 7 * C:7 * C + 16]]
    assert 7 * C + 16 <= szC

    sgr = Cc[:, 0:T]
    ekb = Cc[:, T:2 * T]
    snum = Cc[:, 2 * T:3 * T + 1]
    sden = Cc[:, 3 * T + 1:4 * T + 2]
    hqT = Cc[:, 0:IC * FTB]
    wdqT = Cc[:, IC * FTB: IC * FTB + IC * 128]
    assert IC * FTB + IC * 128 <= szC

    ps = [nc.alloc_psum_tensor(f"ps{i}", [128, 512], F32).ap() for i in range(8)]

    def dsem(name):
        return Sem(nc.alloc_semaphore(name), "", dma=True)
    dSet = dsem("dSet"); dX = dsem("dX"); dWmx = dsem("dWmx")
    dAc = [dsem("dAc0"), dsem("dAc1")]
    dAt = [dsem("dAt0"), dsem("dAt1")]
    dWg = dsem("dWg"); dWu = dsem("dWu"); dWd = dsem("dWd"); dX2 = dsem("dX2")
    dRw = [dsem("dRw0"), dsem("dRw1")]
    dOut = dsem("dOut"); dSA = dsem("dSA"); dSX = dsem("dSX")
    dSQ = dsem("dSQ"); dXQ = dsem("dXQ")
    dCg = dsem("dCg"); dCu = dsem("dCu"); dCd = dsem("dCd")
    all_store_sems = [dOut, dSA, dSX, dSQ, dCg, dCu, dCd]
    tS = Sem(nc.alloc_semaphore("tS"), "tensor")
    gS = Sem(nc.alloc_semaphore("gS"), "gpsimd")
    vS = Sem(nc.alloc_semaphore("vS"), "vector")
    aS = Sem(nc.alloc_semaphore("aS"), "scalar")

    B = Builder()
    arA, arB, arC = Arena("A"), Arena("B"), Arena("C")
    PSB = [Buf(f"psb{i}") for i in range(8)]
    b_iden = Buf("iden"); b_ew = Buf("ew"); b_eu = Buf("eu"); b_tscr = Buf("tscr")
    b_xg = Buf("xg")
    b_lng = [Buf("lng0"), Buf("lng1")]
    b_st = [Buf("st0"), Buf("st1")]
    b_wmix = [Buf("wmix0"), Buf("wmix1"), Buf("wmix2")]
    b_accs = [Buf("accs0"), Buf("accs1")]
    b_attg = [Buf("attg0"), Buf("attg1")]
    b_wgqT = [Buf("wgqT0"), Buf("wgqT1")]
    b_wuqT = [Buf("wuqT0"), Buf("wuqT1")]
    b_qbm = Buf("qbm"); b_qsc = Buf("qsc"); b_qsi = Buf("qsi")
    b_qbm2 = Buf("qbm2"); b_qsc2 = Buf("qsc2"); b_qsi2 = Buf("qsi2")
    b_sacc = [Buf(f"sacc{t}") for t in range(max(1, T // TBA))]
    b_sx2 = Buf("sx2"); b_sxq = Buf("sxq")
    b_wgqc = Buf("wgqc"); b_wuqc = Buf("wuqc"); b_wdqc = Buf("wdqc")

    def load(dst, src, sem, reads=(), writes=(), slow=False):
        return B.op("sync", lambda e, dst=dst, src=src, slow=slow: e.dma_start(
            out=dst, in_=src, allow_slow_non_contiguous=slow),
                    reads=reads, writes=writes, inc=sem, extra=[(sem, sem.n)])

    def store(dst, src, sem, reads=(), writes=()):
        return B.op("gpsimd", lambda e, dst=dst, src=src: e.dma_start(out=dst, in_=src),
                    reads=reads, writes=writes, inc=sem, extra=[(sem, sem.n)])

    def barrier(sem):
        v = sem.n
        B.steps.append(("gpsimd", lambda e, sem=sem, v=v: e.wait_ge(sem.h, v)))

    def vec(fn, reads=(), writes=(), extra=()):
        return B.op("vector", fn, reads, writes, inc=vS, extra=extra)

    def act(fn, reads=(), writes=(), extra=()):
        return B.op("scalar", fn, reads, writes, inc=aS, extra=extra)

    def ten(fn, reads=(), writes=(), extra=()):
        return B.op("tensor", fn, reads, writes, inc=tS, extra=extra)

    def pool(fn, reads=(), writes=(), extra=()):
        return B.op("gpsimd", fn, reads, writes, inc=gS, extra=extra)

    load(iden[:, :], iden_d[:, :], dSet, writes=[b_iden])
    load(tscr[:, :].rearrange("p (d j) -> p d j", d=2),
         tdec[:, :].rearrange("d (j p) -> p d j", p=128), dSet, writes=[b_tscr], slow=True)
    act(lambda e: e.activation(out=ewb[:, :], in_=tscr[:, :], func=AF.Exp),
        reads=[b_tscr], writes=[b_ew])
    vec(lambda e: e.tensor_scalar(out=ewb[:, :], in0=ewb[:, :], scalar1=-1.0,
                                  scalar2=None, op0=AluOpType.mult),
        reads=[b_ew], writes=[b_ew])
    act(lambda e: e.activation(out=ewb[:, :], in_=ewb[:, :], func=AF.Exp),
        reads=[b_ew], writes=[b_ew])
    load(tscr[:, :].rearrange("p (d j) -> p d j", d=2),
         tfir[:, :].rearrange("d (j p) -> p d j", p=128), dSet,
         reads=[b_ew], writes=[b_tscr], slow=True)
    act(lambda e: e.activation(out=eub[:, :], in_=tscr[:, :], func=AF.Exp),
        reads=[b_tscr], writes=[b_eu])
    b_eps = Buf("eps")
    vec(lambda e: e.memset(epsb[:, :], 1e-5), writes=[b_eps])
    b_cmid = Buf("cmid")
    vec(lambda e: e.memset(cmid[:, 7:8], 1.0), writes=[b_cmid])
    vec(lambda e: e.tensor_scalar(out=cmid[:, 7:8], in0=cmid[:, 7:8],
                                  scalar1=2.0 ** -26, scalar2=-1.0,
                                  op0=AluOpType.add, op1=AluOpType.add),
        reads=[b_cmid], writes=[b_cmid])
    vec(lambda e: e.tensor_scalar(out=cmid[:, 6:7], in0=cmid[:, 7:8],
                                  scalar1=0.0, scalar2=None, op0=AluOpType.is_gt),
        reads=[b_cmid], writes=[b_cmid])
    vec(lambda e: e.tensor_scalar(out=cmid[:, 6:7], in0=cmid[:, 6:7],
                                  scalar1=-7e-7, scalar2=7e-7,
                                  op0=AluOpType.mult, op1=AluOpType.add),
        reads=[b_cmid], writes=[b_cmid])
    vec(lambda e: e.scalar_tensor_tensor(out=cmid[:, 7:8], in0=cmid[:, 7:8],
                                         scalar=2.0 ** -24, in1=cmid[:, 6:7],
                                         op0=AluOpType.mult, op1=AluOpType.add),
        reads=[b_cmid], writes=[b_cmid])
    for _mi, _mv in enumerate([0.25, 0.75, 1.25, 1.75, 2.5, 3.5, 5.0]):
        vec(lambda e, _mi=_mi, _mv=_mv: e.tensor_scalar(
            out=cmid[:, _mi:_mi + 1], in0=cmid[:, 7:8], scalar1=_mv, scalar2=_mv,
            op0=AluOpType.mult, op1=AluOpType.add),
            reads=[b_cmid], writes=[b_cmid])
    b_cmidn = Buf("cmidn")
    vec(lambda e: e.tensor_scalar(out=cmidn[:, :], in0=cmid[:, :], scalar1=-1.0,
                                  scalar2=None, op0=AluOpType.mult),
        reads=[b_cmid], writes=[b_cmidn])

    qctr = [0]

    def quant(src_ap, width, b_src, bq, bufs=None):
        Qy, Qn, Qd = bufs if bufs is not None else (q_y, q_n, q_dq)
        assert width <= Qy.shape[1]
        qs = qctr[0] % 2
        qctr[0] += 1
        QB = (q_bm, q_bm2)[qs]
        QS = (q_sc, q_sc2)[qs]
        QI = (q_si, q_si2)[qs]
        BB = (b_qbm, b_qbm2)[qs]
        BS = (b_qsc, b_qsc2)[qs]
        BI = (b_qsi, b_qsi2)[qs]
        nb = width // 16
        src3 = src_ap.rearrange("p (b k) -> p b k", k=16)
        y3 = Qy[:, 0:width].rearrange("p (b k) -> p b k", k=16)
        vec(lambda e: e.tensor_reduce(out=QB[:, 0:nb], in_=src3, axis=AX,
                                      op=AluOpType.max, apply_absolute_value=True),
            reads=[b_src], writes=[BB])
        c6 = float(np.float32(1 / 6))
        vec(lambda e: e.tensor_scalar(out=QS[:, 0:nb], in0=QB[:, 0:nb],
                                      scalar1=c6, scalar2=None, op0=AluOpType.mult),
            reads=[BB], writes=[BS])
        vec(lambda e: e.scalar_tensor_tensor(out=QI[:, 0:nb], in0=QS[:, 0:nb],
                                             scalar=-6.0, in1=QB[:, 0:nb],
                                             op0=AluOpType.mult, op1=AluOpType.add),
            reads=[BS, BB], writes=[BI])
        vec(lambda e: e.scalar_tensor_tensor(out=QS[:, 0:nb], in0=QI[:, 0:nb],
                                             scalar=c6, in1=QS[:, 0:nb],
                                             op0=AluOpType.mult, op1=AluOpType.add),
            reads=[BI, BS], writes=[BS])
        vec(lambda e: e.tensor_scalar(out=QS[:, 0:nb], in0=QS[:, 0:nb],
                                      scalar1=1e-12, scalar2=None, op0=AluOpType.max),
            reads=[BS], writes=[BS])
        vec(lambda e: e.reciprocal(out=QI[:, 0:nb], in_=QS[:, 0:nb]),
            reads=[BS], writes=[BI])
        act(lambda e: e.activation(out=Qy[:, 0:width], in_=src_ap, func=AF.Abs),
            reads=[b_src], writes=[bq["y"]])
        act(lambda e: e.activation(out=Qd[:, 0:width], in_=src_ap, func=AF.Sign),
            reads=[b_src], writes=[bq["dq"]])
        vec(lambda e: e.tensor_tensor(
            out=y3, in0=y3, in1=QI[:, 0:nb].unsqueeze(2).broadcast_to([128, nb, 16]),
            op=AluOpType.mult), reads=[bq["y"], BI], writes=[bq["y"]])
        vec(lambda e: e.tensor_scalar(out=Qn[:, 0:width], in0=Qy[:, 0:width],
                                      scalar1=cmid[:, 0:1], scalar2=None,
                                      op0=AluOpType.is_gt),
            reads=[bq["y"], b_cmid], writes=[bq["n"]])
        for mi in range(1, 7):
            vec(lambda e, mi=mi: e.scalar_tensor_tensor(
                out=Qn[:, 0:width], in0=Qy[:, 0:width], scalar=cmid[:, mi:mi + 1],
                in1=Qn[:, 0:width], op0=AluOpType.is_gt, op1=AluOpType.add),
                reads=[bq["y"], bq["n"], b_cmid], writes=[bq["n"]])
        vec(lambda e: e.tensor_scalar(out=QI[:, 0:nb], in0=QS[:, 0:nb],
                                      scalar1=0.5, scalar2=None, op0=AluOpType.mult),
            reads=[BS], writes=[BI])
        dq3 = Qd[:, 0:width].rearrange("p (b k) -> p b k", k=16)
        vec(lambda e: e.tensor_tensor(
            out=dq3, in0=dq3, in1=QI[:, 0:nb].unsqueeze(2).broadcast_to([128, nb, 16]),
            op=AluOpType.mult), reads=[bq["dq"], BI], writes=[bq["dq"]])
        vec(lambda e: e.tensor_scalar(out=src_ap, in0=Qn[:, 0:width],
                                      scalar1=2.0, scalar2=-4.0,
                                      op0=AluOpType.mult, op1=AluOpType.add),
            reads=[bq["n"]], writes=[b_src])
        vec(lambda e: e.tensor_tensor(out=src_ap, in0=src_ap,
                                      in1=Qn[:, 0:width], op=AluOpType.max),
            reads=[bq["n"], b_src], writes=[b_src])
        vec(lambda e: e.tensor_scalar(out=Qn[:, 0:width], in0=Qn[:, 0:width],
                                      scalar1=4.0, scalar2=-16.0,
                                      op0=AluOpType.mult, op1=AluOpType.add),
            reads=[bq["n"]], writes=[bq["n"]])
        vec(lambda e: e.tensor_tensor(out=Qn[:, 0:width], in0=Qn[:, 0:width],
                                      in1=src_ap, op=AluOpType.max),
            reads=[bq["n"], b_src], writes=[bq["n"]])
        pool(lambda e: e.tensor_tensor(out=Qd[:, 0:width], in0=Qd[:, 0:width],
                                       in1=Qn[:, 0:width], op=AluOpType.mult),
             reads=[bq["dq"], bq["n"]], writes=[bq["dq"]])
        return bq["dq"]

    def transpose(src_ap, b_src, bi, evac_fn, evac_reads, evac_writes, eeng=None):
        ten(lambda e, src_ap=src_ap, bi=bi: e.transpose(ps[bi][:, 0:128], src_ap,
                                                        iden[:, :]),
            reads=[b_src, b_iden], writes=[PSB[bi]])
        (eeng or vec)(lambda e, bi=bi: evac_fn(e, ps[bi][:, 0:128]),
                      reads=[PSB[bi]] + list(evac_reads), writes=list(evac_writes))

    def ln_group(src_ap, b_srcb, dst_ap, b_dst, b_sqd, par=0):
        stp = st2[par]
        sqp = sqd[par]
        b_stp = b_st[par]
        b_sq = b_sqd[par] if isinstance(b_sqd, list) else b_sqd
        vec(lambda e: e.tensor_reduce(out=stp[:, 0:1], in_=src_ap, axis=AX,
                                      op=AluOpType.add),
            reads=[b_srcb], writes=[b_stp])
        act(lambda e: e.activation(out=sqp[:, :], in_=src_ap, func=AF.Square,
                                   accum_out=stp[:, 1:2]),
            reads=[b_srcb], writes=[b_sq, b_stp])
        vec(lambda e: e.tensor_scalar(out=stp[:, 2:3], in0=stp[:, 0:1],
                                      scalar1=1.0 / C, scalar2=None, op0=AluOpType.mult),
            reads=[b_stp], writes=[b_stp])
        vec(lambda e: e.tensor_scalar(out=stp[:, 3:4], in0=stp[:, 1:2],
                                      scalar1=1.0 / C, scalar2=None, op0=AluOpType.mult),
            reads=[b_stp], writes=[b_stp])
        vec(lambda e: e.tensor_tensor(out=stp[:, 4:5], in0=stp[:, 2:3], in1=stp[:, 2:3],
                                      op=AluOpType.mult), reads=[b_stp], writes=[b_stp])
        vec(lambda e: e.tensor_tensor(out=stp[:, 5:6], in0=stp[:, 3:4], in1=stp[:, 4:5],
                                      op=AluOpType.subtract), reads=[b_stp], writes=[b_stp])
        act(lambda e: e.activation(out=stp[:, 6:7], in_=stp[:, 5:6], func=AF.Ln,
                                   bias=epsb[:, :]), reads=[b_stp, b_eps], writes=[b_stp])
        act(lambda e: e.activation(out=stp[:, 7:8], in_=stp[:, 6:7], func=AF.Exp,
                                   scale=-0.5), reads=[b_stp], writes=[b_stp])
        vec(lambda e: e.tensor_scalar(out=dst_ap, in0=src_ap, scalar1=stp[:, 2:3],
                                      scalar2=stp[:, 7:8], op0=AluOpType.subtract,
                                      op1=AluOpType.mult),
            reads=[b_srcb, b_stp], writes=[b_dst])

    def phase0(s, b_lnT, b_sqd):
        for g in range(TG):
            par = g % 2
            xa = xg[:, 0:C]
            load(xa, x_in[s * T + g * 128: s * T + (g + 1) * 128, :], dX, writes=[b_xg])
            ln_group(xa, b_xg, lng2[par][:, :], b_lng[par], b_sqd[par], par=par)
            for kc in range(KC):
                bi = 6 + (kc % 2)
                transpose(lng2[par][:, kc * 128:(kc + 1) * 128], b_lng[par], bi,
                          lambda e, p, kc=kc, g=g: e.copy(
                              out=lnT[:, kc * T + g * 128: kc * T + (g + 1) * 128], in_=p),
                          (), [b_lnT], eeng=act)
        if d_lnT is not None and s == 0:
            store(d_lnT[:, :].rearrange("(kc p) t -> p kc t", p=128),
                  lnT[:, :].rearrange("p (kc t) -> p kc t", kc=KC), dOut, reads=[b_lnT])

    def mixer(s, d, b_lnT, b_of, bx):
        for j in range(KC):
            ekvb = ofb[:, j * T:(j + 1) * T]
            b_ofj = b_of[j]
            for q in range(3):
                dcol = d * 3 * C + q * C + j * 128
                slot = q * KC * 128
                load(wmix[:, slot:slot + KC * 128].rearrange("p (kc m) -> p kc m", kc=KC),
                     rkvw[:, dcol:dcol + 128].rearrange("(kc p) m -> p kc m", p=128),
                     dWmx, writes=[b_wmix[q]])
            for q, evac in ((1, "k"), (2, "v"), (0, "r")):
                for tb in range(NTB):
                    bi = 2 * q + tb % 2
                    def mm(e, q=q, tb=tb, bi=bi):
                        last = None
                        for kc in range(KC):
                            last = e.matmul(
                                ps[bi][:, 0:TBA],
                                wmix[:, q * KC * 128 + kc * 128: q * KC * 128 + (kc + 1) * 128],
                                lnT[:, kc * T + tb * TBA: kc * T + (tb + 1) * TBA],
                                start=(kc == 0), stop=(kc == KC - 1))
                        return last
                    ten(mm, reads=[b_wmix[q], b_lnT], writes=[PSB[bi]])
                    sl = slice(tb * TBA, (tb + 1) * TBA)
                    if evac == "r":
                        act(lambda e, bi=bi, sl=sl: e.activation(
                            out=sgr[:, sl], in_=ps[bi][:, 0:TBA], func=AF.Tanh, scale=0.5),
                            reads=[PSB[bi]], writes=[bx["sgr"]])
                    elif evac == "k":
                        act(lambda e, bi=bi, sl=sl: e.activation(
                            out=ekb[:, sl], in_=ps[bi][:, 0:TBA], func=AF.Exp),
                            reads=[PSB[bi]], writes=[bx["ek"]])
                    else:
                        vec(lambda e, bi=bi, sl=sl, ekvb=ekvb: e.tensor_tensor(
                            out=ekvb[:, sl], in0=ekb[:, sl], in1=ps[bi][:, 0:TBA],
                            op=AluOpType.mult),
                            reads=[PSB[bi], bx["ek"]], writes=[b_ofj])
            ewa = ewb[:, d * KC + j: d * KC + j + 1]
            eua = eub[:, d * KC + j: d * KC + j + 1]
            if d == 0:
                vec(lambda e: e.memset(snum[:, 0:1], 0.0), writes=[bx["sn"]])
                vec(lambda e: e.memset(sden[:, 0:1], 0.0), writes=[bx["sd"]])
                vec(lambda e, ewa=ewa, ekvb=ekvb: e.tensor_tensor_scan(
                    out=snum[:, 1:T + 1], data0=ewa.broadcast_to([128, T]),
                    data1=ekvb, initial=snum[:, 0:1],
                    op0=AluOpType.mult, op1=AluOpType.add),
                    reads=[b_ofj, b_ew, bx["sn"]], writes=[bx["sn"]])
                vec(lambda e, ewa=ewa: e.tensor_tensor_scan(
                    out=sden[:, 1:T + 1], data0=ewa.broadcast_to([128, T]),
                    data1=ekb[:, :], initial=sden[:, 0:1],
                    op0=AluOpType.mult, op1=AluOpType.add),
                    reads=[bx["ek"], b_ew, bx["sd"]], writes=[bx["sd"]])
                nsh, dsh = snum[:, 0:T], sden[:, 0:T]
            else:
                vec(lambda e: e.memset(snum[:, T:T + 1], 0.0), writes=[bx["sn"]])
                vec(lambda e: e.memset(sden[:, T:T + 1], 0.0), writes=[bx["sd"]])
                vec(lambda e, ewa=ewa, ekvb=ekvb: e.tensor_tensor_scan(
                    out=snum[:, 0:T][:, ::-1], data0=ewa.broadcast_to([128, T]),
                    data1=ekvb[:, ::-1], initial=snum[:, T:T + 1],
                    op0=AluOpType.mult, op1=AluOpType.add),
                    reads=[b_ofj, b_ew, bx["sn"]], writes=[bx["sn"]])
                vec(lambda e, ewa=ewa: e.tensor_tensor_scan(
                    out=sden[:, 0:T][:, ::-1], data0=ewa.broadcast_to([128, T]),
                    data1=ekb[:, :][:, ::-1], initial=sden[:, T:T + 1],
                    op0=AluOpType.mult, op1=AluOpType.add),
                    reads=[bx["ek"], b_ew, bx["sd"]], writes=[bx["sd"]])
                nsh, dsh = snum[:, 1:T + 1], sden[:, 1:T + 1]
            vec(lambda e, eua=eua, nsh=nsh, ekvb=ekvb: e.scalar_tensor_tensor(
                out=snum[:, 0:T], in0=ekvb, scalar=eua, in1=nsh,
                op0=AluOpType.mult, op1=AluOpType.add),
                reads=[b_ofj, b_eu, bx["sn"]], writes=[bx["sn"]])
            vec(lambda e, eua=eua, dsh=dsh: e.scalar_tensor_tensor(
                out=sden[:, 0:T], in0=ekb[:, :], scalar=eua, in1=dsh,
                op0=AluOpType.mult, op1=AluOpType.add),
                reads=[bx["ek"], b_eu, bx["sd"]], writes=[bx["sd"]])
            act(lambda e, ekvb=ekvb: e.activation(out=ekvb, in_=sden[:, 0:T], func=AF.Ln),
                reads=[bx["sd"], b_ofj], writes=[b_ofj])
            act(lambda e, ekvb=ekvb: e.activation(out=ekvb, in_=ekvb, func=AF.Exp,
                                       scale=-1.0),
                reads=[b_ofj], writes=[b_ofj])
            vec(lambda e, ekvb=ekvb: e.tensor_tensor(out=ekb[:, :], in0=sden[:, 0:T],
                                          in1=ekvb, op=AluOpType.mult),
                reads=[bx["sd"], b_ofj, bx["ek"]], writes=[bx["ek"]])
            vec(lambda e: e.tensor_scalar(out=ekb[:, :], in0=ekb[:, :],
                                          scalar1=-1.0, scalar2=2.0,
                                          op0=AluOpType.mult, op1=AluOpType.add),
                reads=[bx["ek"]], writes=[bx["ek"]])
            vec(lambda e, ekvb=ekvb: e.tensor_tensor(out=sden[:, 0:T], in0=ekvb,
                                          in1=ekb[:, :], op=AluOpType.mult),
                reads=[b_ofj, bx["ek"], bx["sd"]], writes=[bx["sd"]])
            pool(lambda e: e.tensor_tensor(out=snum[:, 0:T], in0=snum[:, 0:T],
                                           in1=sden[:, 0:T], op=AluOpType.mult),
                 reads=[bx["sn"], bx["sd"]], writes=[bx["sn"]])
            vec(lambda e, j=j: e.scalar_tensor_tensor(
                out=ofb[:, j * T:(j + 1) * T], in0=sgr[:, :], scalar=1.0,
                in1=snum[:, 0:T], op0=AluOpType.add, op1=AluOpType.mult),
                reads=[bx["sgr"], bx["sn"]], writes=[b_ofj])
            yield
        if s == 0:
            dd = d_of if d == 0 else d_ob
            if dd is not None:
                store(dd[:, :].rearrange("(kc p) t -> p kc t", p=128),
                      ofb[:, :].rearrange("p (kc t) -> p kc t", kc=KC), dOut, reads=b_of)

    PREQ = 2 * T >= 4 * C

    def wq_pre(b_of, jobs):
        base = 6 * T
        wq_raw = ofb[:, base:base + C]
        wq_y = ofb[:, base + C:base + 2 * C]
        wq_n = ofb[:, base + 2 * C:base + 3 * C]
        wq_dq = ofb[:, base + 3 * C:base + 4 * C]
        bq6 = {"y": b_of[6], "n": b_of[7], "dq": b_of[7]}
        for (src, dst, w) in jobs:
            load(wq_raw[:, 0:w], src, dWg, writes=[b_of[6]])
            quant(wq_raw[:, 0:w], w, b_of[6], bq6, bufs=(wq_y, wq_n, wq_dq))
            store(dst, wq_dq[:, 0:w], dCg, reads=[b_of[7]], writes=[b_wgqc])
            yield

    def wq_jobs_d0():
        jobs = [(wg_d[i * 128:(i + 1) * 128, :], wgq_r[i, :, :], C)
                for i in range(IC)]
        for mc in range(KC):
            c0 = 0
            while c0 < I:
                w = min(C, I - c0)
                jobs.append((wd_d[mc * 128:(mc + 1) * 128, c0:c0 + w],
                             wdq_r[mc, :, c0:c0 + w], w))
                c0 += w
        return jobs

    def wq_jobs_d1():
        return [(wu_d[i * 128:(i + 1) * 128, :], wuq_r[i, :, :], C)
                for i in range(IC)]

    def outproj_tb(s, d, tb, b_of):
        for m in range(KC):
            mp = m % 3
            msl = mp * KC * 128
            load(wmix[:, msl:msl + KC * 128].rearrange("p (kc m) -> p kc m", kc=KC),
                 outw[d * C:(d + 1) * C, m * 128:(m + 1) * 128].rearrange(
                     "(kc p) m -> p kc m", p=128), dWmx, writes=[b_wmix[mp]])
            vec(lambda e, msl=msl: e.tensor_scalar(
                out=wmix[:, msl:msl + KC * 128], in0=wmix[:, msl:msl + KC * 128],
                scalar1=0.5, scalar2=None, op0=AluOpType.mult),
                reads=[b_wmix[mp]], writes=[b_wmix[mp]])
            bi = m % 2
            def mmo(e, tb=tb, bi=bi, msl=msl):
                last = None
                for kc in range(KC):
                    last = e.matmul(ps[bi][:, 0:TBA],
                                    wmix[:, msl + kc * 128: msl + (kc + 1) * 128],
                                    ofb[:, kc * T + tb * TBA: kc * T + (tb + 1) * TBA],
                                    start=(kc == 0), stop=(kc == KC - 1))
                return last
            ten(mmo, reads=[b_wmix[mp]] + b_of, writes=[PSB[bi]])
            apar = m % 2
            asl = slice(apar * 512, apar * 512 + TBA)
            if d == 0:
                vec(lambda e, bi=bi, asl=asl: e.tensor_copy(
                    out=accs[:, asl], in_=ps[bi][:, 0:TBA]),
                    reads=[PSB[bi]], writes=[b_accs[apar]])
            else:
                load(accs[:, asl],
                     s_acc[m * 128:(m + 1) * 128, tb * TBA:(tb + 1) * TBA],
                     dAc[apar], reads=[b_sacc[tb]], writes=[b_accs[apar]])
                vec(lambda e, bi=bi, asl=asl: e.tensor_tensor(
                    out=accs[:, asl], in0=accs[:, asl], in1=ps[bi][:, 0:TBA],
                    op=AluOpType.add), reads=[PSB[bi], b_accs[apar]],
                    writes=[b_accs[apar]])
            store(s_acc[m * 128:(m + 1) * 128, tb * TBA:(tb + 1) * TBA],
                  accs[:, asl], dSA, reads=[b_accs[apar]], writes=[b_sacc[tb]])
            yield

    def phase2c_tb(s, tba, b_xq, bq, b_sqd):
        for gi in range(TBA // 128):
            g = (tba * TBA) // 128 + gi
            par = g % 2
            sl = (g * 128) // SL
            gl = g - sl * (SL // 128)
            b_xqT = b_xq[sl % 2]
            xqT = xqTs[sl % 2]
            lg = lng2[par]
            bqp = bq[par]
            xa = xg[:, 0:C]
            load(xa, x_in[s * T + g * 128: s * T + (g + 1) * 128, :], dX, writes=[b_xg])
            for kc in range(KC):
                apar = kc % 2
                load(attg[:, apar * 128:(apar + 1) * 128],
                     s_acc[kc * 128:(kc + 1) * 128, g * 128:(g + 1) * 128],
                     dAt[apar], reads=[b_sacc[(g * 128) // TBA]],
                     writes=[b_attg[apar]])
                bi = 6 + (kc % 2)
                ten(lambda e, apar=apar, bi=bi: e.transpose(
                    ps[bi][:, 0:128], attg[:, apar * 128:(apar + 1) * 128], iden[:, :]),
                    reads=[b_attg[apar], b_iden], writes=[PSB[bi]])
                vec(lambda e, bi=bi, kc=kc, xa=xa, lg=lg: e.tensor_tensor(
                    out=lg[:, kc * 128:(kc + 1) * 128], in0=ps[bi][:, 0:128],
                    in1=xa[:, kc * 128:(kc + 1) * 128], op=AluOpType.add),
                    reads=[PSB[bi], b_xg], writes=[b_lng[par]])
            store(s_x2[g * 128:(g + 1) * 128, :], lg[:, :], dSX, reads=[b_lng[par]],
                  writes=[b_sx2])
            if s == 0 and d_x2 is not None:
                store(d_x2[g * 128:(g + 1) * 128, :], lg[:, :], dOut, reads=[b_lng[par]])
            ln_group(lg[:, :], b_lng[par], lg[:, :], b_lng[par], b_sqd, par=par)
            qb = quant(lg[:, :], C, b_lng[par], bqp,
                       bufs=(qc_y[par], qc_n[par], qc_dq[par]))
            yield
            for kc in range(KC):
                bi = 6 + (kc % 2)
                transpose(qc_dq[par][:, kc * 128:(kc + 1) * 128], qb, bi,
                          lambda e, p, kc=kc, gl=gl, xqT=xqT: e.copy(
                              out=xqT[:, kc * SL + gl * 128: kc * SL + (gl + 1) * 128],
                              in_=p), (), [b_xqT], eeng=act)
            if (g + 1) * 128 % SL == 0:
                store(s_xq[:, sl * SL:(sl + 1) * SL].rearrange("(kc p) t -> p kc t", p=128),
                      xqT[:, :].rearrange("p (kc t) -> p kc t", kc=KC),
                      dSQ, reads=[b_xqT], writes=[b_sxq])
            yield


    def p3_gateup(s, sl, tb, k, b_xq, bq, bffn):
        b_hfm = bffn["hfm"]; b_htok = bffn["htok"][k % 2]
        b_wraw = bffn["wraw"]
        b_xqT = b_xq[sl % 2]
        xqT = xqTs[sl % 2]
        first = (s == 0 and sl == 0)
        ftb = first and tb == 0
        t0 = tb * FTB
        hoff = (k % 2) * NG_TB * I
        if tb == 0:
            load(xqT[:, :].rearrange("p (kc t) -> p kc t", kc=KC),
                 s_xq[:, sl * SL:(sl + 1) * SL].rearrange("(kc p) t -> p kc t", p=128),
                 dXQ, reads=[b_sxq], writes=[b_xqT])
        for i in range(IC):
            ip = i % 2
            gsl = ip * KC * 128
            if ftb:
                fam = (() if PREQ else
                       ((wg_d, wgq_c, b_wgqc, wgqT, b_wgqT[ip], dCg, dWg),
                        (wu_d, wuq_c, b_wuqc, wuqT, b_wuqT[ip], dCu, dWu)))
                preq_fam = ((wgq_r, wgq_c, b_wgqc, wgqT, b_wgqT[ip], dCg, dWg),
                            (wuq_r, wuq_c, b_wuqc, wuqT, b_wuqT[ip], dCu, dWu)
                            ) if PREQ else ()
                for (r_dram, cache, b_cache, stb, b_stb, csem, lsem) in preq_fam:
                    rb = wraw[:, ip * C:(ip + 1) * C]
                    load(rb, r_dram[i, :, :], lsem, reads=[b_cache],
                         writes=[b_wraw[ip]])
                    for kc in range(KC):
                        bi = 6 + (kc % 2)
                        transpose(rb[:, kc * 128:(kc + 1) * 128], b_wraw[ip], bi,
                                  lambda e, p, kc=kc, stb=stb, gsl=gsl: e.copy(
                                      out=stb[:, gsl + kc * 128: gsl + (kc + 1) * 128],
                                      in_=p), (), [b_stb], eeng=act)
                    store(cache[i, :, :], stb[:, gsl:gsl + KC * 128], csem,
                          reads=[b_stb], writes=[b_cache])
                for (w_dram, cache, b_cache, stb, b_stb, csem, lsem) in fam:
                    rb = wraw[:, ip * C:(ip + 1) * C]
                    load(rb, w_dram[i * 128:(i + 1) * 128, :], lsem,
                         writes=[b_wraw[ip]])
                    qb = quant(rb, C, b_wraw[ip], bq)
                    for kc in range(KC):
                        bi = 6 + (kc % 2)
                        transpose(q_dq[:, kc * 128:(kc + 1) * 128], qb, bi,
                                  lambda e, p, kc=kc, stb=stb, gsl=gsl: e.copy(
                                      out=stb[:, gsl + kc * 128: gsl + (kc + 1) * 128],
                                      in_=p), (), [b_stb], eeng=act)
                    store(cache[i, :, :], stb[:, gsl:gsl + KC * 128], csem,
                          reads=[b_stb], writes=[b_cache])
            else:
                load(wgqT[:, gsl:gsl + KC * 128], wgq_c[i, :, :], dWg,
                     reads=[b_wgqc], writes=[b_wgqT[ip]])
                load(wuqT[:, gsl:gsl + KC * 128], wuq_c[i, :, :], dWu,
                     reads=[b_wuqc], writes=[b_wuqT[ip]])
            def mmg(e, t0=t0, gsl=gsl, xqT=xqT):
                last = None
                for kc in range(KC):
                    last = e.matmul(ps[0][:, 0:FTB],
                                    wgqT[:, gsl + kc * 128: gsl + (kc + 1) * 128],
                                    xqT[:, kc * SL + t0: kc * SL + t0 + FTB],
                                    start=(kc == 0), stop=(kc == KC - 1))
                return last
            ten(mmg, reads=[b_wgqT[ip], b_xqT], writes=[PSB[0]])
            def mmu(e, t0=t0, gsl=gsl, xqT=xqT):
                last = None
                for kc in range(KC):
                    last = e.matmul(ps[1][:, 0:FTB],
                                    wuqT[:, gsl + kc * 128: gsl + (kc + 1) * 128],
                                    xqT[:, kc * SL + t0: kc * SL + t0 + FTB],
                                    start=(kc == 0), stop=(kc == KC - 1))
                return last
            ten(mmu, reads=[b_wuqT[ip], b_xqT], writes=[PSB[1]])
            hp = i % 2
            hf = hfm2[hp]
            act(lambda e, hf=hf: e.activation(out=hf[:, :], in_=ps[0][:, 0:FTB],
                                              func=AF.Silu),
                reads=[PSB[0]], writes=[b_hfm[hp]])
            vec(lambda e, hf=hf: e.tensor_tensor(out=hf[:, :], in0=hf[:, :],
                                                 in1=ps[1][:, 0:FTB], op=AluOpType.mult),
                reads=[PSB[1], b_hfm[hp]], writes=[b_hfm[hp]])
            for gp in range(NG_TB):
                bi = 4 + gp % 2
                transpose(hf[:, gp * 128:(gp + 1) * 128], b_hfm[hp], bi,
                          lambda e, p, gp=gp, i=i, hoff=hoff: e.copy(
                              out=htok[:, hoff + gp * I + i * 128: hoff + gp * I + (i + 1) * 128],
                              in_=p), (), [b_htok], eeng=act)
            yield

    def p3_rest(s, sl, tb, k, bq, bffn, sub="all"):
        b_x2g = bffn["x2g"]; b_og = bffn["og"]
        b_dsb = bffn["dsb"]; b_htok = bffn["htok"][k % 2]; b_hqT = bffn["hqT"]
        b_wdT = bffn["wdqT"]; b_wraw = bffn["wraw"]
        first = (s == 0 and sl == 0)
        ftb = first and tb == 0
        t0g = s * T + sl * SL + tb * FTB
        t0 = tb * FTB
        hoff = (k % 2) * NG_TB * I
        if sub == "a":
            return
        H1 = IC // 2
        for gp in range(NG_TB):
            for hh in range(2):
                k0 = hh * H1
                kn = (IC - H1) if hh else H1
                qb = quant(htok[:, hoff + gp * I + k0 * 128: hoff + gp * I + (k0 + kn) * 128],
                           kn * 128, b_htok, bq)
                yield
                for kk in range(kn):
                    bi = 4 + (kk % 2)
                    transpose(q_dq[:, kk * 128:(kk + 1) * 128], qb, bi,
                              lambda e, p, kk=kk, k0=k0, gp=gp: e.copy(
                                  out=hqT[:, (k0 + kk) * FTB + gp * 128: (k0 + kk) * FTB + (gp + 1) * 128],
                                  in_=p), (), [b_hqT], eeng=act)
                yield
        for gp in range(NG_TB):
            load(x2g[:, gp * C:(gp + 1) * C],
                 s_x2[sl * SL + t0 + gp * 128: sl * SL + t0 + (gp + 1) * 128, :],
                 dX2, reads=[b_sx2], writes=[b_x2g])
        yield
        if sub == "b":
            return
        for mc in range(KC):
            if ftb:
                for hh in range(2):
                    k0 = hh * H1
                    kn = (IC - H1) if hh else H1
                    rb = wraw[:, 0:kn * 128]
                    if PREQ:
                        load(rb, wdq_r[mc, :, k0 * 128:(k0 + kn) * 128],
                             dWd, reads=[b_wgqc], writes=[b_wraw[0], b_wraw[1]])
                        srcT, bsT = rb, b_wraw[0]
                    else:
                        load(rb, wd_d[mc * 128:(mc + 1) * 128, k0 * 128:(k0 + kn) * 128],
                             dWd, writes=[b_wraw[0], b_wraw[1]])
                        qb = quant(rb, kn * 128, b_wraw[0], bq)
                        srcT, bsT = q_dq, qb
                    for kk in range(kn):
                        bi = 6 + (kk % 2)
                        transpose(srcT[:, kk * 128:(kk + 1) * 128], bsT, bi,
                                  lambda e, p, kk=kk, k0=k0: e.copy(
                                      out=wdqT[:, (k0 + kk) * 128:(k0 + kk + 1) * 128],
                                      in_=p), (), [b_wdT[hh]], eeng=act)
                    store(wdq_c[mc, :, k0 * 128:(k0 + kn) * 128],
                          wdqT[:, k0 * 128:(k0 + kn) * 128], dCd,
                          reads=[b_wdT[hh]], writes=[b_wdqc])
            else:
                for hh in range(2):
                    k0 = hh * H1
                    kn = (IC - H1) if hh else H1
                    load(wdqT[:, k0 * 128:(k0 + kn) * 128].rearrange(
                            "p (kc m) -> p kc m", kc=kn),
                         wdq_c[mc, :, k0 * 128:(k0 + kn) * 128].rearrange(
                            "p (kc m) -> p kc m", kc=kn),
                         dWd, reads=[b_wdqc], writes=[b_wdT[hh]])
            bi = 2 + mc % 2
            for hh in range(2):
                k0 = hh * H1
                kn = (IC - H1) if hh else H1
                def mmd(e, bi=bi, k0=k0, kn=kn):
                    last = None
                    for kk in range(kn):
                        kc = k0 + kk
                        last = e.matmul(ps[bi][:, 0:FTB],
                                        wdqT[:, kc * 128:(kc + 1) * 128],
                                        hqT[:, kc * FTB:(kc + 1) * FTB],
                                        start=(kc == 0), stop=(kc == IC - 1))
                    return last
                ten(mmd, reads=[b_wdT[hh], b_hqT], writes=[PSB[bi]])
            dp = mc % 2
            dsp = dsb2[dp]
            vec(lambda e, bi=bi, dsp=dsp: e.tensor_copy(out=dsp[:, :], in_=ps[bi][:, 0:FTB]),
                reads=[PSB[bi]], writes=[b_dsb[dp]])
            for gp in range(NG_TB):
                bi2 = 6 + gp % 2
                ten(lambda e, gp=gp, bi2=bi2, dsp=dsp: e.transpose(
                    ps[bi2][:, 0:128], dsp[:, gp * 128:(gp + 1) * 128], iden[:, :]),
                    reads=[b_dsb[dp], b_iden], writes=[PSB[bi2]])
                vec(lambda e, gp=gp, mc=mc, bi2=bi2: e.tensor_tensor(
                    out=og[:, gp * C + mc * 128: gp * C + (mc + 1) * 128],
                    in0=ps[bi2][:, 0:128],
                    in1=x2g[:, gp * C + mc * 128: gp * C + (mc + 1) * 128],
                    op=AluOpType.add),
                    reads=[PSB[bi2], b_x2g], writes=[b_og])
            yield
        for gp in range(NG_TB):
            store(out_d[t0g + gp * 128: t0g + (gp + 1) * 128, :],
                  og[:, gp * C:(gp + 1) * C], dOut, reads=[b_og])
        yield

    def pump(g, n=1):
        if g is None:
            return False
        got = False
        for _ in range(n):
            try:
                next(g)
                got = True
            except StopIteration:
                break
        return got

    def drain(g):
        if g is not None:
            for _ in g:
                pass

    for s in range(BLOC):
        b_lnT = arA.role(f"lnT{s}")
        b_sqd = [arC.role(f"sqd{s}a"), arC.role(f"sqd{s}b")]
        phase0(s, b_lnT, b_sqd)
        if stage == "p0":
            break
        arC.new_era()
        bx = {kk: arC.role(f"{kk}{s}") for kk in ("sgr", "ek", "ekv", "sn", "sd")}
        b_of = [arB.role(f"of{s}_{j}") for j in range(KC)]
        mx = mixer(s, 0, b_lnT, b_of, bx)
        wq = (wq_pre(b_of, wq_jobs_d0())
              if (s == 0 and PREQ and stage == "full") else None)
        for jj in range(KC):
            if jj == 6:
                drain(wq)
            pump(mx, 1)
            pump(wq, 8)
        drain(mx)
        drain(wq)
        if stage != "p1":
            for tb in range(NTB):
                drain(outproj_tb(s, 0, tb, b_of))
        if stage in ("p2", "p2c", "full"):
            mx2 = mixer(s, 1, b_lnT, b_of, bx)
            wq2 = (wq_pre(b_of, wq_jobs_d1())
                   if (s == 0 and PREQ and stage == "full") else None)
            for jj in range(KC):
                if jj == 6:
                    drain(wq2)
                pump(mx2, 1)
                pump(wq2, 4)
            drain(mx2)
            drain(wq2)
            barrier(dSA)
            arA.new_era(); arC.new_era()
            b_xq = [arA.role(f"xq{s}_0"), arA.role(f"xq{s}_1")]
            bqc = [{"y": arC.role(f"qcy{s}a"), "n": arC.role(f"qcn{s}a"),
                    "dq": arC.role(f"qcdq{s}a")},
                   {"y": arC.role(f"qcy{s}b"), "n": arC.role(f"qcn{s}b"),
                    "dq": arC.role(f"qcdq{s}b")}]
            b_sqd2 = [bqc[0]["dq"], bqc[1]["dq"]]
            drain(outproj_tb(s, 1, 0, b_of))
            for tb in range(NTB):
                op = outproj_tb(s, 1, tb + 1, b_of) if tb + 1 < NTB else None
                pc = (phase2c_tb(s, tb, b_xq, bqc, b_sqd2)
                      if stage in ("p2c", "full") else None)
                for gi in range(TBA // 128):
                    pump(pc, 1)
                    pump(op, 2)
                    pump(pc, 1)
                drain(op)
                drain(pc)
        if stage == "full":
            barrier(dSX)
            arB.new_era(); arC.new_era()
            bq = {"y": arB.role(f"qy{s}"), "n": arB.role(f"qn{s}"),
                  "dq": arB.role(f"qdq{s}")}
            bffn = {"x2g": arA.role(f"x2g{s}"), "og": arA.role(f"og{s}"),
                    "hfm": [arA.role(f"hfm{s}a"), arA.role(f"hfm{s}b")],
                    "dsb": [arA.role(f"dsb{s}a"), arA.role(f"dsb{s}b")],
                    "htok": [arB.role(f"htok{s}a"), arB.role(f"htok{s}b")],
                    "hqT": arC.role(f"hqT{s}"),
                    "wdqT": [arC.role(f"wdqT{s}a"), arC.role(f"wdqT{s}b")],
                    "wraw": [arA.role(f"wraw{s}a"), arA.role(f"wraw{s}b")]}
            sub = {"p3a": "a", "p3b": "b"}.get(stage0, "all")
            kts = [(sl, tb) for sl in range(NSLAB) for tb in range(NFTB)]
            drain(p3_gateup(s, kts[0][0], kts[0][1], 0, b_xq, bq, bffn))
            for k in range(len(kts)):
                gu = (p3_gateup(s, kts[k + 1][0], kts[k + 1][1], k + 1,
                                b_xq, bq, bffn) if k + 1 < len(kts) else None)
                rs = p3_rest(s, kts[k][0], kts[k][1], k, bq, bffn, sub=sub)
                for _ in range(2 * NG_TB):
                    pump(rs, 1)
                    pump(gu, 3)
                    pump(rs, 1)
                while True:
                    a = pump(gu, 1)
                    b = pump(rs, 1)
                    if not a and not b:
                        break
            if s == 0:
                barrier(dCg); barrier(dCu); barrier(dCd)
        arA.new_era(); arB.new_era(); arC.new_era()
        if stage not in ("full", "p2", "p2c"):
            break
        if stage != "full":
            break

    for s_ in all_store_sems:
        v_ = s_.n
        if v_:
            B.steps.append(("gpsimd", lambda e, s_=s_, v_=v_: e.wait_ge(s_.h, v_)))

    with nc.Block() as block:
        @block.sync
        def _(eng):
            for name, step in B.steps:
                if name == "sync":
                    step(eng)

        @block.tensor
        def _(eng):
            for name, step in B.steps:
                if name == "tensor":
                    step(eng)

        @block.vector
        def _(eng):
            for name, step in B.steps:
                if name == "vector":
                    step(eng)

        @block.scalar
        def _(eng):
            for name, step in B.steps:
                if name == "scalar":
                    step(eng)

        @block.gpsimd
        def _(eng):
            for name, step in B.steps:
                if name == "gpsimd":
                    step(eng)

    return nc, dbg_aps


def host_inputs(x, rkv_w, out_w, td, tf, tdr, tfr, wg, wu, wd):
    return {
        "x": np.ascontiguousarray(x.reshape(-1, C)),
        "rkvw_t": np.ascontiguousarray(rkv_w.T),
        "outw_t": np.ascontiguousarray(out_w.T),
        "wg": np.ascontiguousarray(wg),
        "wu": np.ascontiguousarray(wu),
        "wd": np.ascontiguousarray(wd),
        "tdec": np.ascontiguousarray(np.stack([td, tdr])),
        "tfir": np.ascontiguousarray(np.stack([tf, tfr])),
        "iden": np.eye(128, dtype=np.float32),
    }


from concourse.bass_utils import run_bass_kernel_spmd

_NCORES = 8
_T = 2048
_BLOC = 2


def kernel(L0_attn_x, L0_attn_rkv_w, L0_attn_out_w, L0_attn_time_decay,
           L0_attn_time_first, L0_attn_time_decay_rev, L0_attn_time_first_rev,
           L0_ffn_w_gate, L0_ffn_w_up, L0_ffn_w_down):
    x = np.ascontiguousarray(np.asarray(L0_attn_x, dtype=np.float32))
    Bfull = x.shape[0]
    shared = {
        "rkvw_t": np.ascontiguousarray(np.asarray(L0_attn_rkv_w, np.float32).T),
        "outw_t": np.ascontiguousarray(np.asarray(L0_attn_out_w, np.float32).T),
        "wg": np.ascontiguousarray(np.asarray(L0_ffn_w_gate, np.float32)),
        "wu": np.ascontiguousarray(np.asarray(L0_ffn_w_up, np.float32)),
        "wd": np.ascontiguousarray(np.asarray(L0_ffn_w_down, np.float32)),
        "tdec": np.ascontiguousarray(np.stack([np.asarray(L0_attn_time_decay, np.float32),
                                               np.asarray(L0_attn_time_decay_rev, np.float32)])),
        "tfir": np.ascontiguousarray(np.stack([np.asarray(L0_attn_time_first, np.float32),
                                               np.asarray(L0_attn_time_first_rev, np.float32)])),
        "iden": np.eye(128, dtype=np.float32),
    }
    nc = bass.Bass("TRN2", target_bir_lowering=False)
    build(nc, T=_T, BLOC=_BLOC, dbg=(), stage="full")
    in_maps = []
    for c in range(_NCORES):
        m = dict(shared)
        m["x"] = np.ascontiguousarray(
            x[c * _BLOC:(c + 1) * _BLOC].reshape(_BLOC * _T, C))
        in_maps.append(m)
    res = run_bass_kernel_spmd(nc, in_maps, core_ids=list(range(_NCORES)))
    out = np.empty((Bfull, _T, C), dtype=np.float32)
    for c in range(_NCORES):
        out[c * _BLOC:(c + 1) * _BLOC] = np.asarray(
            res.results[c]["out"], dtype=np.float32).reshape(_BLOC, _T, C)
    return out


# version 22
# speedup vs baseline: 1.0656x; 1.0656x over previous
import numpy as np
import concourse.bass as bass
import concourse.mybir as mybir
from concourse.alu_op_type import AluOpType

F32 = mybir.dt.float32
AF = mybir.ActivationFunctionType
AX = mybir.AxisListType.X

C = 1024
KC = C // 128
I = 2816
IC = I // 128


class Sem:
    def __init__(self, handle, engine, dma=False):
        self.h = handle
        self.engine = engine
        self.n = 0
        self.step = 16 if dma else 1


class Buf:
    __slots__ = ("name", "w", "r")

    def __init__(self, name, seed=None):
        self.name = name
        self.w = dict(seed.w) if seed else {}
        self.r = dict(seed.r) if seed else {}
        if seed:
            for s, v in seed.r.items():
                if self.w.get(s, -1) < v:
                    self.w[s] = v

    @staticmethod
    def merge(d, sem, val):
        if val is not None and d.get(sem, -1) < val:
            d[sem] = val


class Arena:
    def __init__(self, name):
        self.name = name
        self.tail = Buf(name + "_tail")
        self.live = []

    def role(self, name):
        b = Buf(name, seed=self.tail)
        self.live.append(b)
        return b

    def new_era(self):
        for b in self.live:
            for s, v in b.w.items():
                Buf.merge(self.tail.w, s, v)
            for s, v in b.r.items():
                Buf.merge(self.tail.r, s, v)
        self.live = []


class Builder:
    def __init__(self):
        self.steps = []
        self.waited = {}

    def op(self, engine, fn, reads=(), writes=(), inc=None, extra=()):
        waits = {}
        def note(s, v):
            if s.step == 16:
                v = s.n
            if v and waits.get(s, -1) < v:
                waits[s] = v
        for b in reads:
            for s, v in b.w.items():
                note(s, v)
        for b in writes:
            for d in (b.w, b.r):
                for s, v in d.items():
                    note(s, v)
        for s, v in extra:
            note(s, v)
        val = None
        if inc is not None:
            inc.n += inc.step
            val = inc.n
        wl = []
        for s, v in sorted(waits.items(), key=lambda x: id(x[0])):
            key = (engine, s)
            if self.waited.get(key, -1) >= v:
                continue
            self.waited[key] = v
            wl.append((s, v))

        def step(eng, fn=fn, wl=wl, inc=inc):
            for s, v in wl:
                eng.wait_ge(s.h, v)
            last = fn(eng)
            if inc is not None:
                assert last is not None, "op with inc= must return an instruction"
                last.then_inc(inc.h, inc.step)

        self.steps.append((engine, step))
        if inc is not None:
            for b in reads:
                Buf.merge(b.r, inc, val)
            for b in writes:
                Buf.merge(b.w, inc, val)
        return (inc, val)


def build(nc, T=2048, BLOC=2, dbg=(), stage="full"):
    stage0 = stage
    stage = {"p3a": "full", "p3b": "full"}.get(stage, stage)
    dbg = set(dbg)
    TG = T // 128
    TBA = min(512, T)
    NTB = T // TBA
    SL = max(128, min(512, T // 4))
    NSLAB = T // SL
    FTB = min(256, SL)
    NFTB = SL // FTB
    NG_TB = FTB // 128
    TS = SL

    x_in = nc.dram_tensor("x", [BLOC * T, C], F32, kind="ExternalInput").ap()
    rkvw = nc.dram_tensor("rkvw_t", [C, 6 * C], F32, kind="ExternalInput").ap()
    outw = nc.dram_tensor("outw_t", [2 * C, C], F32, kind="ExternalInput").ap()
    wg_d = nc.dram_tensor("wg", [I, C], F32, kind="ExternalInput").ap()
    wu_d = nc.dram_tensor("wu", [I, C], F32, kind="ExternalInput").ap()
    wd_d = nc.dram_tensor("wd", [C, I], F32, kind="ExternalInput").ap()
    tdec = nc.dram_tensor("tdec", [2, C], F32, kind="ExternalInput").ap()
    tfir = nc.dram_tensor("tfir", [2, C], F32, kind="ExternalInput").ap()
    iden_d = nc.dram_tensor("iden", [128, 128], F32, kind="ExternalInput").ap()
    out_d = nc.dram_tensor("out", [BLOC * T, C], F32, kind="ExternalOutput").ap()
    wgq_c = nc.dram_tensor("wgq_cache", [IC, 128, KC * 128], F32, kind="Internal").ap()
    wgq_r = nc.dram_tensor("wgq_raw", [IC, 128, C], F32, kind="Internal").ap()
    wuq_r = nc.dram_tensor("wuq_raw", [IC, 128, C], F32, kind="Internal").ap()
    wdq_r = nc.dram_tensor("wdq_raw", [KC, 128, I], F32, kind="Internal").ap()
    wuq_c = nc.dram_tensor("wuq_cache", [IC, 128, KC * 128], F32, kind="Internal").ap()
    wdq_c = nc.dram_tensor("wdq_cache", [KC, 128, IC * 128], F32, kind="Internal").ap()
    s_acc = nc.dram_tensor("s_acc", [C, T], F32, kind="Internal").ap()
    s_x2 = nc.dram_tensor("s_x2", [T, C], F32, kind="Internal").ap()
    s_xq = nc.dram_tensor("s_xq", [C, T], F32, kind="Internal").ap()

    dbg_aps = {}
    def mkdbg(name, shape):
        if name in dbg:
            dbg_aps[name] = nc.dram_tensor("dbg_" + name, shape, F32,
                                           kind="ExternalOutput").ap()
        return dbg_aps.get(name)
    d_lnT = mkdbg("lnT", [C, T])
    d_of = mkdbg("of", [C, T])
    d_ob = mkdbg("ob", [C, T])
    d_x2 = mkdbg("x2", [T, C])
    d_xqT = mkdbg("xqT", [C, SL])
    d_wgq = mkdbg("wgq", [I, C])
    d_wdq = mkdbg("wdq", [I, C])
    d_hq = mkdbg("hq", [FTB, I])

    used_kb = [0.0]
    def sb(name, cols):
        used_kb[0] += cols * 4 / 1024
        return nc.alloc_sbuf_tensor(name, [128, cols], F32).ap()

    szA = max(KC * T, 2 * KC * SL + 2 * NG_TB * C + 4 * FTB + 2 * C)
    szB = max(KC * T, 5 * I + C)
    szC = max(4 * T + 2, IC * FTB + IC * 128, 7 * C + 16)
    A = sb("arA", szA)
    Bb = sb("arB", szB)
    Cc = sb("arC", szC)
    iden = sb("s_iden", 128)
    ewb = sb("s_ew", 2 * KC)
    eub = sb("s_eu", 2 * KC)
    tscr = sb("s_tscr", 2 * KC)
    xg = sb("s_xg", C)
    lng = sb("s_lng", C)
    st = sb("s_st", 16)
    wmix = sb("s_wmix", 3 * KC * 128)
    accs = sb("s_accs", 2 * 512)
    attg = sb("s_attg", 2 * 128)
    wgqT = sb("s_wgqT", 2 * KC * 128)
    wuqT = sb("s_wuqT", 2 * KC * 128)
    q_bm = sb("s_qbm", I // 16)
    q_sc = sb("s_qsc", I // 16)
    q_si = sb("s_qsi", I // 16)
    q_bm2 = sb("s_qbm2", I // 16)
    q_sc2 = sb("s_qsc2", I // 16)
    q_si2 = sb("s_qsi2", I // 16)
    epsb = sb("s_eps", 1)
    cmid = sb("s_cmid", 8)
    cmidn = sb("s_cmidn", 8)
    assert used_kb[0] < 214, f"SBUF {used_kb[0]:.1f}KB"

    lnT = A[:, 0:KC * T]
    xqTs = [A[:, k * KC * SL:(k + 1) * KC * SL] for k in range(2)]
    _o = 2 * KC * SL
    x2g = A[:, _o: _o + NG_TB * C]
    og = A[:, _o + NG_TB * C: _o + 2 * NG_TB * C]
    hfm2 = [A[:, _o + 2 * NG_TB * C + k * FTB: _o + 2 * NG_TB * C + (k + 1) * FTB] for k in range(2)]
    dsb2 = [A[:, _o + 2 * NG_TB * C + (2 + k) * FTB: _o + 2 * NG_TB * C + (3 + k) * FTB] for k in range(2)]
    _p3_end = _o + 2 * NG_TB * C + 4 * FTB
    wraw = A[:, _p3_end:_p3_end + 2 * C]
    assert _p3_end + 2 * C <= szA

    QW = 1408
    ofb = Bb[:, 0:KC * T]
    q_y = Bb[:, 0:QW]
    q_n = Bb[:, QW:2 * QW]
    q_dq = Bb[:, 2 * QW:3 * QW]
    htok = Bb[:, 3 * QW:3 * QW + 2 * NG_TB * I]
    assert 3 * QW + 2 * NG_TB * I <= szB
    qc_y = [Cc[:, 0:C], Cc[:, 3 * C:4 * C]]
    qc_n = [Cc[:, C:2 * C], Cc[:, 4 * C:5 * C]]
    qc_dq = [Cc[:, 2 * C:3 * C], Cc[:, 5 * C:6 * C]]
    sqd = [Cc[:, 2 * C:3 * C], Cc[:, 5 * C:6 * C]]
    lng2 = [lng, Cc[:, 6 * C:7 * C]]
    st2 = [st, Cc[:, 7 * C:7 * C + 16]]
    assert 7 * C + 16 <= szC

    sgr = Cc[:, 0:T]
    ekb = Cc[:, T:2 * T]
    snum = Cc[:, 2 * T:3 * T + 1]
    sden = Cc[:, 3 * T + 1:4 * T + 2]
    hqT = Cc[:, 0:IC * FTB]
    wdqT = Cc[:, IC * FTB: IC * FTB + IC * 128]
    assert IC * FTB + IC * 128 <= szC

    ps = [nc.alloc_psum_tensor(f"ps{i}", [128, 512], F32).ap() for i in range(8)]

    def dsem(name):
        return Sem(nc.alloc_semaphore(name), "", dma=True)
    dSet = dsem("dSet"); dX = dsem("dX"); dWmx = dsem("dWmx")
    dAc = [dsem("dAc0"), dsem("dAc1")]
    dAt = [dsem("dAt0"), dsem("dAt1")]
    dWg = dsem("dWg"); dWu = dsem("dWu"); dWd = dsem("dWd"); dX2 = dsem("dX2")
    dRw = [dsem("dRw0"), dsem("dRw1")]
    dOut = dsem("dOut"); dSA = dsem("dSA"); dSX = dsem("dSX")
    dSQ = dsem("dSQ"); dXQ = dsem("dXQ")
    dCg = dsem("dCg"); dCu = dsem("dCu"); dCd = dsem("dCd")
    all_store_sems = [dOut, dSA, dSX, dSQ, dCg, dCu, dCd]
    tS = Sem(nc.alloc_semaphore("tS"), "tensor")
    gS = Sem(nc.alloc_semaphore("gS"), "gpsimd")
    vS = Sem(nc.alloc_semaphore("vS"), "vector")
    aS = Sem(nc.alloc_semaphore("aS"), "scalar")

    B = Builder()
    arA, arB, arC = Arena("A"), Arena("B"), Arena("C")
    PSB = [Buf(f"psb{i}") for i in range(8)]
    b_iden = Buf("iden"); b_ew = Buf("ew"); b_eu = Buf("eu"); b_tscr = Buf("tscr")
    b_xg = Buf("xg")
    b_lng = [Buf("lng0"), Buf("lng1")]
    b_st = [Buf("st0"), Buf("st1")]
    b_wmix = [Buf("wmix0"), Buf("wmix1"), Buf("wmix2")]
    b_accs = [Buf("accs0"), Buf("accs1")]
    b_attg = [Buf("attg0"), Buf("attg1")]
    b_wgqT = [Buf("wgqT0"), Buf("wgqT1")]
    b_wuqT = [Buf("wuqT0"), Buf("wuqT1")]
    b_qbm = Buf("qbm"); b_qsc = Buf("qsc"); b_qsi = Buf("qsi")
    b_qbm2 = Buf("qbm2"); b_qsc2 = Buf("qsc2"); b_qsi2 = Buf("qsi2")
    b_sacc = [Buf(f"sacc{t}") for t in range(max(1, T // TBA))]
    b_sx2 = Buf("sx2"); b_sxq = Buf("sxq")
    b_wgqc = Buf("wgqc"); b_wuqc = Buf("wuqc"); b_wdqc = Buf("wdqc")

    def load(dst, src, sem, reads=(), writes=(), slow=False):
        return B.op("sync", lambda e, dst=dst, src=src, slow=slow: e.dma_start(
            out=dst, in_=src, allow_slow_non_contiguous=slow),
                    reads=reads, writes=writes, inc=sem, extra=[(sem, sem.n)])

    def store(dst, src, sem, reads=(), writes=()):
        return B.op("gpsimd", lambda e, dst=dst, src=src: e.dma_start(out=dst, in_=src),
                    reads=reads, writes=writes, inc=sem, extra=[(sem, sem.n)])

    def barrier(sem):
        v = sem.n
        B.steps.append(("gpsimd", lambda e, sem=sem, v=v: e.wait_ge(sem.h, v)))

    def vec(fn, reads=(), writes=(), extra=()):
        return B.op("vector", fn, reads, writes, inc=vS, extra=extra)

    def act(fn, reads=(), writes=(), extra=()):
        return B.op("scalar", fn, reads, writes, inc=aS, extra=extra)

    def ten(fn, reads=(), writes=(), extra=()):
        return B.op("tensor", fn, reads, writes, inc=tS, extra=extra)

    def pool(fn, reads=(), writes=(), extra=()):
        return B.op("gpsimd", fn, reads, writes, inc=gS, extra=extra)

    load(iden[:, :], iden_d[:, :], dSet, writes=[b_iden])
    load(tscr[:, :].rearrange("p (d j) -> p d j", d=2),
         tdec[:, :].rearrange("d (j p) -> p d j", p=128), dSet, writes=[b_tscr], slow=True)
    act(lambda e: e.activation(out=ewb[:, :], in_=tscr[:, :], func=AF.Exp),
        reads=[b_tscr], writes=[b_ew])
    vec(lambda e: e.tensor_scalar(out=ewb[:, :], in0=ewb[:, :], scalar1=-1.0,
                                  scalar2=None, op0=AluOpType.mult),
        reads=[b_ew], writes=[b_ew])
    act(lambda e: e.activation(out=ewb[:, :], in_=ewb[:, :], func=AF.Exp),
        reads=[b_ew], writes=[b_ew])
    load(tscr[:, :].rearrange("p (d j) -> p d j", d=2),
         tfir[:, :].rearrange("d (j p) -> p d j", p=128), dSet,
         reads=[b_ew], writes=[b_tscr], slow=True)
    act(lambda e: e.activation(out=eub[:, :], in_=tscr[:, :], func=AF.Exp),
        reads=[b_tscr], writes=[b_eu])
    b_eps = Buf("eps")
    vec(lambda e: e.memset(epsb[:, :], 1e-5), writes=[b_eps])
    b_cmid = Buf("cmid")
    vec(lambda e: e.memset(cmid[:, 7:8], 1.0), writes=[b_cmid])
    vec(lambda e: e.tensor_scalar(out=cmid[:, 7:8], in0=cmid[:, 7:8],
                                  scalar1=2.0 ** -26, scalar2=-1.0,
                                  op0=AluOpType.add, op1=AluOpType.add),
        reads=[b_cmid], writes=[b_cmid])
    vec(lambda e: e.tensor_scalar(out=cmid[:, 6:7], in0=cmid[:, 7:8],
                                  scalar1=0.0, scalar2=None, op0=AluOpType.is_gt),
        reads=[b_cmid], writes=[b_cmid])
    vec(lambda e: e.tensor_scalar(out=cmid[:, 6:7], in0=cmid[:, 6:7],
                                  scalar1=-7e-7, scalar2=7e-7,
                                  op0=AluOpType.mult, op1=AluOpType.add),
        reads=[b_cmid], writes=[b_cmid])
    vec(lambda e: e.scalar_tensor_tensor(out=cmid[:, 7:8], in0=cmid[:, 7:8],
                                         scalar=2.0 ** -24, in1=cmid[:, 6:7],
                                         op0=AluOpType.mult, op1=AluOpType.add),
        reads=[b_cmid], writes=[b_cmid])
    for _mi, _mv in enumerate([0.25, 0.75, 1.25, 1.75, 2.5, 3.5, 5.0]):
        vec(lambda e, _mi=_mi, _mv=_mv: e.tensor_scalar(
            out=cmid[:, _mi:_mi + 1], in0=cmid[:, 7:8], scalar1=_mv, scalar2=_mv,
            op0=AluOpType.mult, op1=AluOpType.add),
            reads=[b_cmid], writes=[b_cmid])
    b_cmidn = Buf("cmidn")
    vec(lambda e: e.tensor_scalar(out=cmidn[:, :], in0=cmid[:, :], scalar1=-1.0,
                                  scalar2=None, op0=AluOpType.mult),
        reads=[b_cmid], writes=[b_cmidn])

    qctr = [0]

    def quant(src_ap, width, b_src, bq, bufs=None):
        Qy, Qn, Qd = bufs if bufs is not None else (q_y, q_n, q_dq)
        assert width <= Qy.shape[1]
        qs = qctr[0] % 2
        qctr[0] += 1
        QB = (q_bm, q_bm2)[qs]
        QS = (q_sc, q_sc2)[qs]
        QI = (q_si, q_si2)[qs]
        BB = (b_qbm, b_qbm2)[qs]
        BS = (b_qsc, b_qsc2)[qs]
        BI = (b_qsi, b_qsi2)[qs]
        nb = width // 16
        src3 = src_ap.rearrange("p (b k) -> p b k", k=16)
        y3 = Qy[:, 0:width].rearrange("p (b k) -> p b k", k=16)
        vec(lambda e: e.tensor_reduce(out=QB[:, 0:nb], in_=src3, axis=AX,
                                      op=AluOpType.max, apply_absolute_value=True),
            reads=[b_src], writes=[BB])
        c6 = float(np.float32(1 / 6))
        vec(lambda e: e.tensor_scalar(out=QS[:, 0:nb], in0=QB[:, 0:nb],
                                      scalar1=c6, scalar2=None, op0=AluOpType.mult),
            reads=[BB], writes=[BS])
        vec(lambda e: e.scalar_tensor_tensor(out=QI[:, 0:nb], in0=QS[:, 0:nb],
                                             scalar=-6.0, in1=QB[:, 0:nb],
                                             op0=AluOpType.mult, op1=AluOpType.add),
            reads=[BS, BB], writes=[BI])
        vec(lambda e: e.scalar_tensor_tensor(out=QS[:, 0:nb], in0=QI[:, 0:nb],
                                             scalar=c6, in1=QS[:, 0:nb],
                                             op0=AluOpType.mult, op1=AluOpType.add),
            reads=[BI, BS], writes=[BS])
        vec(lambda e: e.tensor_scalar(out=QS[:, 0:nb], in0=QS[:, 0:nb],
                                      scalar1=1e-12, scalar2=None, op0=AluOpType.max),
            reads=[BS], writes=[BS])
        vec(lambda e: e.reciprocal(out=QI[:, 0:nb], in_=QS[:, 0:nb]),
            reads=[BS], writes=[BI])
        act(lambda e: e.activation(out=Qy[:, 0:width], in_=src_ap, func=AF.Abs),
            reads=[b_src], writes=[bq["y"]])
        act(lambda e: e.activation(out=Qd[:, 0:width], in_=src_ap, func=AF.Sign),
            reads=[b_src], writes=[bq["dq"]])
        vec(lambda e: e.tensor_tensor(
            out=y3, in0=y3, in1=QI[:, 0:nb].unsqueeze(2).broadcast_to([128, nb, 16]),
            op=AluOpType.mult), reads=[bq["y"], BI], writes=[bq["y"]])
        vec(lambda e: e.tensor_scalar(out=Qn[:, 0:width], in0=Qy[:, 0:width],
                                      scalar1=cmid[:, 0:1], scalar2=None,
                                      op0=AluOpType.is_gt),
            reads=[bq["y"], b_cmid], writes=[bq["n"]])
        for mi in range(1, 7):
            vec(lambda e, mi=mi: e.scalar_tensor_tensor(
                out=Qn[:, 0:width], in0=Qy[:, 0:width], scalar=cmid[:, mi:mi + 1],
                in1=Qn[:, 0:width], op0=AluOpType.is_gt, op1=AluOpType.add),
                reads=[bq["y"], bq["n"], b_cmid], writes=[bq["n"]])
        vec(lambda e: e.tensor_scalar(out=QI[:, 0:nb], in0=QS[:, 0:nb],
                                      scalar1=0.5, scalar2=None, op0=AluOpType.mult),
            reads=[BS], writes=[BI])
        dq3 = Qd[:, 0:width].rearrange("p (b k) -> p b k", k=16)
        vec(lambda e: e.tensor_tensor(
            out=dq3, in0=dq3, in1=QI[:, 0:nb].unsqueeze(2).broadcast_to([128, nb, 16]),
            op=AluOpType.mult), reads=[bq["dq"], BI], writes=[bq["dq"]])
        vec(lambda e: e.tensor_scalar(out=src_ap, in0=Qn[:, 0:width],
                                      scalar1=2.0, scalar2=-4.0,
                                      op0=AluOpType.mult, op1=AluOpType.add),
            reads=[bq["n"]], writes=[b_src])
        vec(lambda e: e.tensor_tensor(out=src_ap, in0=src_ap,
                                      in1=Qn[:, 0:width], op=AluOpType.max),
            reads=[bq["n"], b_src], writes=[b_src])
        vec(lambda e: e.tensor_scalar(out=Qn[:, 0:width], in0=Qn[:, 0:width],
                                      scalar1=4.0, scalar2=-16.0,
                                      op0=AluOpType.mult, op1=AluOpType.add),
            reads=[bq["n"]], writes=[bq["n"]])
        vec(lambda e: e.tensor_tensor(out=Qn[:, 0:width], in0=Qn[:, 0:width],
                                      in1=src_ap, op=AluOpType.max),
            reads=[bq["n"], b_src], writes=[bq["n"]])
        pool(lambda e: e.tensor_tensor(out=Qd[:, 0:width], in0=Qd[:, 0:width],
                                       in1=Qn[:, 0:width], op=AluOpType.mult),
             reads=[bq["dq"], bq["n"]], writes=[bq["dq"]])
        return bq["dq"]

    def transpose(src_ap, b_src, bi, evac_fn, evac_reads, evac_writes, eeng=None):
        ten(lambda e, src_ap=src_ap, bi=bi: e.transpose(ps[bi][:, 0:128], src_ap,
                                                        iden[:, :]),
            reads=[b_src, b_iden], writes=[PSB[bi]])
        (eeng or vec)(lambda e, bi=bi: evac_fn(e, ps[bi][:, 0:128]),
                      reads=[PSB[bi]] + list(evac_reads), writes=list(evac_writes))

    def ln_group(src_ap, b_srcb, dst_ap, b_dst, b_sqd, par=0):
        stp = st2[par]
        sqp = sqd[par]
        b_stp = b_st[par]
        b_sq = b_sqd[par] if isinstance(b_sqd, list) else b_sqd
        vec(lambda e: e.tensor_reduce(out=stp[:, 0:1], in_=src_ap, axis=AX,
                                      op=AluOpType.add),
            reads=[b_srcb], writes=[b_stp])
        act(lambda e: e.activation(out=sqp[:, :], in_=src_ap, func=AF.Square,
                                   accum_out=stp[:, 1:2]),
            reads=[b_srcb], writes=[b_sq, b_stp])
        vec(lambda e: e.tensor_scalar(out=stp[:, 2:3], in0=stp[:, 0:1],
                                      scalar1=1.0 / C, scalar2=None, op0=AluOpType.mult),
            reads=[b_stp], writes=[b_stp])
        vec(lambda e: e.tensor_scalar(out=stp[:, 3:4], in0=stp[:, 1:2],
                                      scalar1=1.0 / C, scalar2=None, op0=AluOpType.mult),
            reads=[b_stp], writes=[b_stp])
        vec(lambda e: e.tensor_tensor(out=stp[:, 4:5], in0=stp[:, 2:3], in1=stp[:, 2:3],
                                      op=AluOpType.mult), reads=[b_stp], writes=[b_stp])
        vec(lambda e: e.tensor_tensor(out=stp[:, 5:6], in0=stp[:, 3:4], in1=stp[:, 4:5],
                                      op=AluOpType.subtract), reads=[b_stp], writes=[b_stp])
        act(lambda e: e.activation(out=stp[:, 6:7], in_=stp[:, 5:6], func=AF.Ln,
                                   bias=epsb[:, :]), reads=[b_stp, b_eps], writes=[b_stp])
        act(lambda e: e.activation(out=stp[:, 7:8], in_=stp[:, 6:7], func=AF.Exp,
                                   scale=-0.5), reads=[b_stp], writes=[b_stp])
        vec(lambda e: e.tensor_scalar(out=dst_ap, in0=src_ap, scalar1=stp[:, 2:3],
                                      scalar2=stp[:, 7:8], op0=AluOpType.subtract,
                                      op1=AluOpType.mult),
            reads=[b_srcb, b_stp], writes=[b_dst])

    def phase0(s, b_lnT, b_sqd):
        for g in range(TG):
            par = g % 2
            xa = xg[:, 0:C]
            load(xa, x_in[s * T + g * 128: s * T + (g + 1) * 128, :], dX, writes=[b_xg])
            ln_group(xa, b_xg, lng2[par][:, :], b_lng[par], b_sqd[par], par=par)
            for kc in range(KC):
                bi = 6 + (kc % 2)
                transpose(lng2[par][:, kc * 128:(kc + 1) * 128], b_lng[par], bi,
                          lambda e, p, kc=kc, g=g: e.copy(
                              out=lnT[:, kc * T + g * 128: kc * T + (g + 1) * 128], in_=p),
                          (), [b_lnT], eeng=act)
        if d_lnT is not None and s == 0:
            store(d_lnT[:, :].rearrange("(kc p) t -> p kc t", p=128),
                  lnT[:, :].rearrange("p (kc t) -> p kc t", kc=KC), dOut, reads=[b_lnT])

    def mixer(s, d, b_lnT, b_of, bx):
        for j in range(KC):
            ekvb = ofb[:, j * T:(j + 1) * T]
            b_ofj = b_of[j]
            for q in range(3):
                dcol = d * 3 * C + q * C + j * 128
                slot = q * KC * 128
                load(wmix[:, slot:slot + KC * 128].rearrange("p (kc m) -> p kc m", kc=KC),
                     rkvw[:, dcol:dcol + 128].rearrange("(kc p) m -> p kc m", p=128),
                     dWmx, writes=[b_wmix[q]])
            for q, evac in ((1, "k"), (2, "v"), (0, "r")):
                for tb in range(NTB):
                    bi = 2 * q + tb % 2
                    def mm(e, q=q, tb=tb, bi=bi):
                        last = None
                        for kc in range(KC):
                            last = e.matmul(
                                ps[bi][:, 0:TBA],
                                wmix[:, q * KC * 128 + kc * 128: q * KC * 128 + (kc + 1) * 128],
                                lnT[:, kc * T + tb * TBA: kc * T + (tb + 1) * TBA],
                                start=(kc == 0), stop=(kc == KC - 1))
                        return last
                    ten(mm, reads=[b_wmix[q], b_lnT], writes=[PSB[bi]])
                    sl = slice(tb * TBA, (tb + 1) * TBA)
                    if evac == "r":
                        act(lambda e, bi=bi, sl=sl: e.activation(
                            out=sgr[:, sl], in_=ps[bi][:, 0:TBA], func=AF.Tanh, scale=0.5),
                            reads=[PSB[bi]], writes=[bx["sgr"]])
                    elif evac == "k":
                        act(lambda e, bi=bi, sl=sl: e.activation(
                            out=ekb[:, sl], in_=ps[bi][:, 0:TBA], func=AF.Exp),
                            reads=[PSB[bi]], writes=[bx["ek"]])
                    else:
                        vec(lambda e, bi=bi, sl=sl, ekvb=ekvb: e.tensor_tensor(
                            out=ekvb[:, sl], in0=ekb[:, sl], in1=ps[bi][:, 0:TBA],
                            op=AluOpType.mult),
                            reads=[PSB[bi], bx["ek"]], writes=[b_ofj])
            ewa = ewb[:, d * KC + j: d * KC + j + 1]
            eua = eub[:, d * KC + j: d * KC + j + 1]
            if d == 0:
                vec(lambda e: e.memset(snum[:, 0:1], 0.0), writes=[bx["sn"]])
                vec(lambda e: e.memset(sden[:, 0:1], 0.0), writes=[bx["sd"]])
                vec(lambda e, ewa=ewa, ekvb=ekvb: e.tensor_tensor_scan(
                    out=snum[:, 1:T + 1], data0=ewa.broadcast_to([128, T]),
                    data1=ekvb, initial=snum[:, 0:1],
                    op0=AluOpType.mult, op1=AluOpType.add),
                    reads=[b_ofj, b_ew, bx["sn"]], writes=[bx["sn"]])
                vec(lambda e, ewa=ewa: e.tensor_tensor_scan(
                    out=sden[:, 1:T + 1], data0=ewa.broadcast_to([128, T]),
                    data1=ekb[:, :], initial=sden[:, 0:1],
                    op0=AluOpType.mult, op1=AluOpType.add),
                    reads=[bx["ek"], b_ew, bx["sd"]], writes=[bx["sd"]])
                nsh, dsh = snum[:, 0:T], sden[:, 0:T]
            else:
                vec(lambda e: e.memset(snum[:, T:T + 1], 0.0), writes=[bx["sn"]])
                vec(lambda e: e.memset(sden[:, T:T + 1], 0.0), writes=[bx["sd"]])
                vec(lambda e, ewa=ewa, ekvb=ekvb: e.tensor_tensor_scan(
                    out=snum[:, 0:T][:, ::-1], data0=ewa.broadcast_to([128, T]),
                    data1=ekvb[:, ::-1], initial=snum[:, T:T + 1],
                    op0=AluOpType.mult, op1=AluOpType.add),
                    reads=[b_ofj, b_ew, bx["sn"]], writes=[bx["sn"]])
                vec(lambda e, ewa=ewa: e.tensor_tensor_scan(
                    out=sden[:, 0:T][:, ::-1], data0=ewa.broadcast_to([128, T]),
                    data1=ekb[:, :][:, ::-1], initial=sden[:, T:T + 1],
                    op0=AluOpType.mult, op1=AluOpType.add),
                    reads=[bx["ek"], b_ew, bx["sd"]], writes=[bx["sd"]])
                nsh, dsh = snum[:, 1:T + 1], sden[:, 1:T + 1]
            vec(lambda e, eua=eua, nsh=nsh, ekvb=ekvb: e.scalar_tensor_tensor(
                out=snum[:, 0:T], in0=ekvb, scalar=eua, in1=nsh,
                op0=AluOpType.mult, op1=AluOpType.add),
                reads=[b_ofj, b_eu, bx["sn"]], writes=[bx["sn"]])
            vec(lambda e, eua=eua, dsh=dsh: e.scalar_tensor_tensor(
                out=sden[:, 0:T], in0=ekb[:, :], scalar=eua, in1=dsh,
                op0=AluOpType.mult, op1=AluOpType.add),
                reads=[bx["ek"], b_eu, bx["sd"]], writes=[bx["sd"]])
            act(lambda e, ekvb=ekvb: e.activation(out=ekvb, in_=sden[:, 0:T], func=AF.Ln),
                reads=[bx["sd"], b_ofj], writes=[b_ofj])
            act(lambda e, ekvb=ekvb: e.activation(out=ekvb, in_=ekvb, func=AF.Exp,
                                       scale=-1.0),
                reads=[b_ofj], writes=[b_ofj])
            vec(lambda e, ekvb=ekvb: e.tensor_tensor(out=ekb[:, :], in0=sden[:, 0:T],
                                          in1=ekvb, op=AluOpType.mult),
                reads=[bx["sd"], b_ofj, bx["ek"]], writes=[bx["ek"]])
            vec(lambda e: e.tensor_scalar(out=ekb[:, :], in0=ekb[:, :],
                                          scalar1=-1.0, scalar2=2.0,
                                          op0=AluOpType.mult, op1=AluOpType.add),
                reads=[bx["ek"]], writes=[bx["ek"]])
            vec(lambda e, ekvb=ekvb: e.tensor_tensor(out=sden[:, 0:T], in0=ekvb,
                                          in1=ekb[:, :], op=AluOpType.mult),
                reads=[b_ofj, bx["ek"], bx["sd"]], writes=[bx["sd"]])
            pool(lambda e: e.tensor_tensor(out=snum[:, 0:T], in0=snum[:, 0:T],
                                           in1=sden[:, 0:T], op=AluOpType.mult),
                 reads=[bx["sn"], bx["sd"]], writes=[bx["sn"]])
            vec(lambda e, j=j: e.scalar_tensor_tensor(
                out=ofb[:, j * T:(j + 1) * T], in0=sgr[:, :], scalar=1.0,
                in1=snum[:, 0:T], op0=AluOpType.add, op1=AluOpType.mult),
                reads=[bx["sgr"], bx["sn"]], writes=[b_ofj])
            yield
        if s == 0:
            dd = d_of if d == 0 else d_ob
            if dd is not None:
                store(dd[:, :].rearrange("(kc p) t -> p kc t", p=128),
                      ofb[:, :].rearrange("p (kc t) -> p kc t", kc=KC), dOut, reads=b_of)

    PREQ = 2 * T >= 4 * C

    def wq_pre(b_of, jobs):
        base = 6 * T
        wq_raw = ofb[:, base:base + C]
        wq_y = ofb[:, base + C:base + 2 * C]
        wq_n = ofb[:, base + 2 * C:base + 3 * C]
        wq_dq = ofb[:, base + 3 * C:base + 4 * C]
        bq6 = {"y": b_of[6], "n": b_of[7], "dq": b_of[7]}
        for (src, dst, w) in jobs:
            load(wq_raw[:, 0:w], src, dWg, writes=[b_of[6]])
            quant(wq_raw[:, 0:w], w, b_of[6], bq6, bufs=(wq_y, wq_n, wq_dq))
            store(dst, wq_dq[:, 0:w], dCg, reads=[b_of[7]], writes=[b_wgqc])
            yield

    PREQ_WD = False

    def wq_jobs_d0():
        jobs = [(wg_d[i * 128:(i + 1) * 128, :], wgq_r[i, :, :], C)
                for i in range(IC)]
        if PREQ_WD:
            for mc in range(KC):
                c0 = 0
                while c0 < I:
                    w = min(C, I - c0)
                    jobs.append((wd_d[mc * 128:(mc + 1) * 128, c0:c0 + w],
                                 wdq_r[mc, :, c0:c0 + w], w))
                    c0 += w
        return jobs

    def wq_jobs_d1():
        return [(wu_d[i * 128:(i + 1) * 128, :], wuq_r[i, :, :], C)
                for i in range(IC)]

    def outproj_tb(s, d, tb, b_of):
        for m in range(KC):
            mp = m % 3
            msl = mp * KC * 128
            load(wmix[:, msl:msl + KC * 128].rearrange("p (kc m) -> p kc m", kc=KC),
                 outw[d * C:(d + 1) * C, m * 128:(m + 1) * 128].rearrange(
                     "(kc p) m -> p kc m", p=128), dWmx, writes=[b_wmix[mp]])
            vec(lambda e, msl=msl: e.tensor_scalar(
                out=wmix[:, msl:msl + KC * 128], in0=wmix[:, msl:msl + KC * 128],
                scalar1=0.5, scalar2=None, op0=AluOpType.mult),
                reads=[b_wmix[mp]], writes=[b_wmix[mp]])
            bi = m % 2
            def mmo(e, tb=tb, bi=bi, msl=msl):
                last = None
                for kc in range(KC):
                    last = e.matmul(ps[bi][:, 0:TBA],
                                    wmix[:, msl + kc * 128: msl + (kc + 1) * 128],
                                    ofb[:, kc * T + tb * TBA: kc * T + (tb + 1) * TBA],
                                    start=(kc == 0), stop=(kc == KC - 1))
                return last
            ten(mmo, reads=[b_wmix[mp]] + b_of, writes=[PSB[bi]])
            apar = m % 2
            asl = slice(apar * 512, apar * 512 + TBA)
            if d == 0:
                vec(lambda e, bi=bi, asl=asl: e.tensor_copy(
                    out=accs[:, asl], in_=ps[bi][:, 0:TBA]),
                    reads=[PSB[bi]], writes=[b_accs[apar]])
            else:
                load(accs[:, asl],
                     s_acc[m * 128:(m + 1) * 128, tb * TBA:(tb + 1) * TBA],
                     dAc[apar], reads=[b_sacc[tb]], writes=[b_accs[apar]])
                vec(lambda e, bi=bi, asl=asl: e.tensor_tensor(
                    out=accs[:, asl], in0=accs[:, asl], in1=ps[bi][:, 0:TBA],
                    op=AluOpType.add), reads=[PSB[bi], b_accs[apar]],
                    writes=[b_accs[apar]])
            store(s_acc[m * 128:(m + 1) * 128, tb * TBA:(tb + 1) * TBA],
                  accs[:, asl], dSA, reads=[b_accs[apar]], writes=[b_sacc[tb]])
            yield

    def phase2c_tb(s, tba, b_xq, bq, b_sqd):
        for gi in range(TBA // 128):
            g = (tba * TBA) // 128 + gi
            par = g % 2
            sl = (g * 128) // SL
            gl = g - sl * (SL // 128)
            b_xqT = b_xq[sl % 2]
            xqT = xqTs[sl % 2]
            lg = lng2[par]
            bqp = bq[par]
            xa = xg[:, 0:C]
            load(xa, x_in[s * T + g * 128: s * T + (g + 1) * 128, :], dX, writes=[b_xg])
            for kc in range(KC):
                apar = kc % 2
                load(attg[:, apar * 128:(apar + 1) * 128],
                     s_acc[kc * 128:(kc + 1) * 128, g * 128:(g + 1) * 128],
                     dAt[apar], reads=[b_sacc[(g * 128) // TBA]],
                     writes=[b_attg[apar]])
                bi = 6 + (kc % 2)
                ten(lambda e, apar=apar, bi=bi: e.transpose(
                    ps[bi][:, 0:128], attg[:, apar * 128:(apar + 1) * 128], iden[:, :]),
                    reads=[b_attg[apar], b_iden], writes=[PSB[bi]])
                vec(lambda e, bi=bi, kc=kc, xa=xa, lg=lg: e.tensor_tensor(
                    out=lg[:, kc * 128:(kc + 1) * 128], in0=ps[bi][:, 0:128],
                    in1=xa[:, kc * 128:(kc + 1) * 128], op=AluOpType.add),
                    reads=[PSB[bi], b_xg], writes=[b_lng[par]])
            store(s_x2[g * 128:(g + 1) * 128, :], lg[:, :], dSX, reads=[b_lng[par]],
                  writes=[b_sx2])
            if s == 0 and d_x2 is not None:
                store(d_x2[g * 128:(g + 1) * 128, :], lg[:, :], dOut, reads=[b_lng[par]])
            ln_group(lg[:, :], b_lng[par], lg[:, :], b_lng[par], b_sqd, par=par)
            qb = quant(lg[:, :], C, b_lng[par], bqp,
                       bufs=(qc_y[par], qc_n[par], qc_dq[par]))
            yield
            for kc in range(KC):
                bi = 6 + (kc % 2)
                transpose(qc_dq[par][:, kc * 128:(kc + 1) * 128], qb, bi,
                          lambda e, p, kc=kc, gl=gl, xqT=xqT: e.copy(
                              out=xqT[:, kc * SL + gl * 128: kc * SL + (gl + 1) * 128],
                              in_=p), (), [b_xqT], eeng=act)
            if (g + 1) * 128 % SL == 0:
                store(s_xq[:, sl * SL:(sl + 1) * SL].rearrange("(kc p) t -> p kc t", p=128),
                      xqT[:, :].rearrange("p (kc t) -> p kc t", kc=KC),
                      dSQ, reads=[b_xqT], writes=[b_sxq])
            yield


    def p3_gateup(s, sl, tb, k, b_xq, bq, bffn):
        b_hfm = bffn["hfm"]; b_htok = bffn["htok"][k % 2]
        b_wraw = bffn["wraw"]
        b_xqT = b_xq[sl % 2]
        xqT = xqTs[sl % 2]
        first = (s == 0 and sl == 0)
        ftb = first and tb == 0
        t0 = tb * FTB
        hoff = (k % 2) * NG_TB * I
        if tb == 0:
            load(xqT[:, :].rearrange("p (kc t) -> p kc t", kc=KC),
                 s_xq[:, sl * SL:(sl + 1) * SL].rearrange("(kc p) t -> p kc t", p=128),
                 dXQ, reads=[b_sxq], writes=[b_xqT])
        for i in range(IC):
            ip = i % 2
            gsl = ip * KC * 128
            if ftb:
                fam = (() if PREQ else
                       ((wg_d, wgq_c, b_wgqc, wgqT, b_wgqT[ip], dCg, dWg),
                        (wu_d, wuq_c, b_wuqc, wuqT, b_wuqT[ip], dCu, dWu)))
                preq_fam = ((wgq_r, wgq_c, b_wgqc, wgqT, b_wgqT[ip], dCg, dWg),
                            (wuq_r, wuq_c, b_wuqc, wuqT, b_wuqT[ip], dCu, dWu)
                            ) if PREQ else ()
                for (r_dram, cache, b_cache, stb, b_stb, csem, lsem) in preq_fam:
                    rb = wraw[:, ip * C:(ip + 1) * C]
                    load(rb, r_dram[i, :, :], lsem, reads=[b_cache],
                         writes=[b_wraw[ip]])
                    for kc in range(KC):
                        bi = 6 + (kc % 2)
                        transpose(rb[:, kc * 128:(kc + 1) * 128], b_wraw[ip], bi,
                                  lambda e, p, kc=kc, stb=stb, gsl=gsl: e.copy(
                                      out=stb[:, gsl + kc * 128: gsl + (kc + 1) * 128],
                                      in_=p), (), [b_stb], eeng=act)
                    store(cache[i, :, :], stb[:, gsl:gsl + KC * 128], csem,
                          reads=[b_stb], writes=[b_cache])
                for (w_dram, cache, b_cache, stb, b_stb, csem, lsem) in fam:
                    rb = wraw[:, ip * C:(ip + 1) * C]
                    load(rb, w_dram[i * 128:(i + 1) * 128, :], lsem,
                         writes=[b_wraw[ip]])
                    qb = quant(rb, C, b_wraw[ip], bq)
                    for kc in range(KC):
                        bi = 6 + (kc % 2)
                        transpose(q_dq[:, kc * 128:(kc + 1) * 128], qb, bi,
                                  lambda e, p, kc=kc, stb=stb, gsl=gsl: e.copy(
                                      out=stb[:, gsl + kc * 128: gsl + (kc + 1) * 128],
                                      in_=p), (), [b_stb], eeng=act)
                    store(cache[i, :, :], stb[:, gsl:gsl + KC * 128], csem,
                          reads=[b_stb], writes=[b_cache])
            else:
                load(wgqT[:, gsl:gsl + KC * 128], wgq_c[i, :, :], dWg,
                     reads=[b_wgqc], writes=[b_wgqT[ip]])
                load(wuqT[:, gsl:gsl + KC * 128], wuq_c[i, :, :], dWu,
                     reads=[b_wuqc], writes=[b_wuqT[ip]])
            def mmg(e, t0=t0, gsl=gsl, xqT=xqT):
                last = None
                for kc in range(KC):
                    last = e.matmul(ps[0][:, 0:FTB],
                                    wgqT[:, gsl + kc * 128: gsl + (kc + 1) * 128],
                                    xqT[:, kc * SL + t0: kc * SL + t0 + FTB],
                                    start=(kc == 0), stop=(kc == KC - 1))
                return last
            ten(mmg, reads=[b_wgqT[ip], b_xqT], writes=[PSB[0]])
            def mmu(e, t0=t0, gsl=gsl, xqT=xqT):
                last = None
                for kc in range(KC):
                    last = e.matmul(ps[1][:, 0:FTB],
                                    wuqT[:, gsl + kc * 128: gsl + (kc + 1) * 128],
                                    xqT[:, kc * SL + t0: kc * SL + t0 + FTB],
                                    start=(kc == 0), stop=(kc == KC - 1))
                return last
            ten(mmu, reads=[b_wuqT[ip], b_xqT], writes=[PSB[1]])
            hp = i % 2
            hf = hfm2[hp]
            act(lambda e, hf=hf: e.activation(out=hf[:, :], in_=ps[0][:, 0:FTB],
                                              func=AF.Silu),
                reads=[PSB[0]], writes=[b_hfm[hp]])
            vec(lambda e, hf=hf: e.tensor_tensor(out=hf[:, :], in0=hf[:, :],
                                                 in1=ps[1][:, 0:FTB], op=AluOpType.mult),
                reads=[PSB[1], b_hfm[hp]], writes=[b_hfm[hp]])
            for gp in range(NG_TB):
                bi = 4 + gp % 2
                transpose(hf[:, gp * 128:(gp + 1) * 128], b_hfm[hp], bi,
                          lambda e, p, gp=gp, i=i, hoff=hoff: e.copy(
                              out=htok[:, hoff + gp * I + i * 128: hoff + gp * I + (i + 1) * 128],
                              in_=p), (), [b_htok], eeng=act)
            yield

    def p3_rest(s, sl, tb, k, bq, bffn, sub="all"):
        b_x2g = bffn["x2g"]; b_og = bffn["og"]
        b_dsb = bffn["dsb"]; b_htok = bffn["htok"][k % 2]; b_hqT = bffn["hqT"]
        b_wdT = bffn["wdqT"]; b_wraw = bffn["wraw"]
        first = (s == 0 and sl == 0)
        ftb = first and tb == 0
        t0g = s * T + sl * SL + tb * FTB
        t0 = tb * FTB
        hoff = (k % 2) * NG_TB * I
        if sub == "a":
            return
        H1 = IC // 2
        for gp in range(NG_TB):
            for hh in range(2):
                k0 = hh * H1
                kn = (IC - H1) if hh else H1
                qb = quant(htok[:, hoff + gp * I + k0 * 128: hoff + gp * I + (k0 + kn) * 128],
                           kn * 128, b_htok, bq)
                yield
                for kk in range(kn):
                    bi = 4 + (kk % 2)
                    transpose(q_dq[:, kk * 128:(kk + 1) * 128], qb, bi,
                              lambda e, p, kk=kk, k0=k0, gp=gp: e.copy(
                                  out=hqT[:, (k0 + kk) * FTB + gp * 128: (k0 + kk) * FTB + (gp + 1) * 128],
                                  in_=p), (), [b_hqT], eeng=act)
                yield
        for gp in range(NG_TB):
            load(x2g[:, gp * C:(gp + 1) * C],
                 s_x2[sl * SL + t0 + gp * 128: sl * SL + t0 + (gp + 1) * 128, :],
                 dX2, reads=[b_sx2], writes=[b_x2g])
        yield
        if sub == "b":
            return
        for mc in range(KC):
            if ftb:
                for hh in range(2):
                    k0 = hh * H1
                    kn = (IC - H1) if hh else H1
                    rb = wraw[:, 0:kn * 128]
                    if PREQ and PREQ_WD:
                        load(rb, wdq_r[mc, :, k0 * 128:(k0 + kn) * 128],
                             dWd, reads=[b_wgqc], writes=[b_wraw[0], b_wraw[1]])
                        srcT, bsT = rb, b_wraw[0]
                    else:
                        load(rb, wd_d[mc * 128:(mc + 1) * 128, k0 * 128:(k0 + kn) * 128],
                             dWd, writes=[b_wraw[0], b_wraw[1]])
                        qb = quant(rb, kn * 128, b_wraw[0], bq)
                        srcT, bsT = q_dq, qb
                    for kk in range(kn):
                        bi = 6 + (kk % 2)
                        transpose(srcT[:, kk * 128:(kk + 1) * 128], bsT, bi,
                                  lambda e, p, kk=kk, k0=k0: e.copy(
                                      out=wdqT[:, (k0 + kk) * 128:(k0 + kk + 1) * 128],
                                      in_=p), (), [b_wdT[hh]], eeng=act)
                    store(wdq_c[mc, :, k0 * 128:(k0 + kn) * 128],
                          wdqT[:, k0 * 128:(k0 + kn) * 128], dCd,
                          reads=[b_wdT[hh]], writes=[b_wdqc])
            else:
                for hh in range(2):
                    k0 = hh * H1
                    kn = (IC - H1) if hh else H1
                    load(wdqT[:, k0 * 128:(k0 + kn) * 128].rearrange(
                            "p (kc m) -> p kc m", kc=kn),
                         wdq_c[mc, :, k0 * 128:(k0 + kn) * 128].rearrange(
                            "p (kc m) -> p kc m", kc=kn),
                         dWd, reads=[b_wdqc], writes=[b_wdT[hh]])
            bi = 2 + mc % 2
            for hh in range(2):
                k0 = hh * H1
                kn = (IC - H1) if hh else H1
                def mmd(e, bi=bi, k0=k0, kn=kn):
                    last = None
                    for kk in range(kn):
                        kc = k0 + kk
                        last = e.matmul(ps[bi][:, 0:FTB],
                                        wdqT[:, kc * 128:(kc + 1) * 128],
                                        hqT[:, kc * FTB:(kc + 1) * FTB],
                                        start=(kc == 0), stop=(kc == IC - 1))
                    return last
                ten(mmd, reads=[b_wdT[hh], b_hqT], writes=[PSB[bi]])
            dp = mc % 2
            dsp = dsb2[dp]
            vec(lambda e, bi=bi, dsp=dsp: e.tensor_copy(out=dsp[:, :], in_=ps[bi][:, 0:FTB]),
                reads=[PSB[bi]], writes=[b_dsb[dp]])
            for gp in range(NG_TB):
                bi2 = 6 + gp % 2
                ten(lambda e, gp=gp, bi2=bi2, dsp=dsp: e.transpose(
                    ps[bi2][:, 0:128], dsp[:, gp * 128:(gp + 1) * 128], iden[:, :]),
                    reads=[b_dsb[dp], b_iden], writes=[PSB[bi2]])
                vec(lambda e, gp=gp, mc=mc, bi2=bi2: e.tensor_tensor(
                    out=og[:, gp * C + mc * 128: gp * C + (mc + 1) * 128],
                    in0=ps[bi2][:, 0:128],
                    in1=x2g[:, gp * C + mc * 128: gp * C + (mc + 1) * 128],
                    op=AluOpType.add),
                    reads=[PSB[bi2], b_x2g], writes=[b_og])
            yield
        for gp in range(NG_TB):
            store(out_d[t0g + gp * 128: t0g + (gp + 1) * 128, :],
                  og[:, gp * C:(gp + 1) * C], dOut, reads=[b_og])
        yield

    def pump(g, n=1):
        if g is None:
            return False
        got = False
        for _ in range(n):
            try:
                next(g)
                got = True
            except StopIteration:
                break
        return got

    def drain(g):
        if g is not None:
            for _ in g:
                pass

    for s in range(BLOC):
        b_lnT = arA.role(f"lnT{s}")
        b_sqd = [arC.role(f"sqd{s}a"), arC.role(f"sqd{s}b")]
        phase0(s, b_lnT, b_sqd)
        if stage == "p0":
            break
        arC.new_era()
        bx = {kk: arC.role(f"{kk}{s}") for kk in ("sgr", "ek", "ekv", "sn", "sd")}
        b_of = [arB.role(f"of{s}_{j}") for j in range(KC)]
        mx = mixer(s, 0, b_lnT, b_of, bx)
        wq = (wq_pre(b_of, wq_jobs_d0())
              if (s == 0 and PREQ and stage == "full") else None)
        for jj in range(KC):
            if jj == 6:
                drain(wq)
            pump(mx, 1)
            pump(wq, 4)
        drain(mx)
        drain(wq)
        if stage != "p1":
            for tb in range(NTB):
                drain(outproj_tb(s, 0, tb, b_of))
        if stage in ("p2", "p2c", "full"):
            mx2 = mixer(s, 1, b_lnT, b_of, bx)
            wq2 = (wq_pre(b_of, wq_jobs_d1())
                   if (s == 0 and PREQ and stage == "full") else None)
            for jj in range(KC):
                if jj == 6:
                    drain(wq2)
                pump(mx2, 1)
                pump(wq2, 4)
            drain(mx2)
            drain(wq2)
            barrier(dSA)
            arA.new_era(); arC.new_era()
            b_xq = [arA.role(f"xq{s}_0"), arA.role(f"xq{s}_1")]
            bqc = [{"y": arC.role(f"qcy{s}a"), "n": arC.role(f"qcn{s}a"),
                    "dq": arC.role(f"qcdq{s}a")},
                   {"y": arC.role(f"qcy{s}b"), "n": arC.role(f"qcn{s}b"),
                    "dq": arC.role(f"qcdq{s}b")}]
            b_sqd2 = [bqc[0]["dq"], bqc[1]["dq"]]
            drain(outproj_tb(s, 1, 0, b_of))
            for tb in range(NTB):
                op = outproj_tb(s, 1, tb + 1, b_of) if tb + 1 < NTB else None
                pc = (phase2c_tb(s, tb, b_xq, bqc, b_sqd2)
                      if stage in ("p2c", "full") else None)
                for gi in range(TBA // 128):
                    pump(pc, 1)
                    pump(op, 2)
                    pump(pc, 1)
                drain(op)
                drain(pc)
        if stage == "full":
            barrier(dSX)
            arB.new_era(); arC.new_era()
            bq = {"y": arB.role(f"qy{s}"), "n": arB.role(f"qn{s}"),
                  "dq": arB.role(f"qdq{s}")}
            bffn = {"x2g": arA.role(f"x2g{s}"), "og": arA.role(f"og{s}"),
                    "hfm": [arA.role(f"hfm{s}a"), arA.role(f"hfm{s}b")],
                    "dsb": [arA.role(f"dsb{s}a"), arA.role(f"dsb{s}b")],
                    "htok": [arB.role(f"htok{s}a"), arB.role(f"htok{s}b")],
                    "hqT": arC.role(f"hqT{s}"),
                    "wdqT": [arC.role(f"wdqT{s}a"), arC.role(f"wdqT{s}b")],
                    "wraw": [arA.role(f"wraw{s}a"), arA.role(f"wraw{s}b")]}
            sub = {"p3a": "a", "p3b": "b"}.get(stage0, "all")
            kts = [(sl, tb) for sl in range(NSLAB) for tb in range(NFTB)]
            drain(p3_gateup(s, kts[0][0], kts[0][1], 0, b_xq, bq, bffn))
            for k in range(len(kts)):
                gu = (p3_gateup(s, kts[k + 1][0], kts[k + 1][1], k + 1,
                                b_xq, bq, bffn) if k + 1 < len(kts) else None)
                rs = p3_rest(s, kts[k][0], kts[k][1], k, bq, bffn, sub=sub)
                for _ in range(2 * NG_TB):
                    pump(rs, 1)
                    pump(gu, 3)
                    pump(rs, 1)
                while True:
                    a = pump(gu, 1)
                    b = pump(rs, 1)
                    if not a and not b:
                        break
            if s == 0:
                barrier(dCg); barrier(dCu); barrier(dCd)
        arA.new_era(); arB.new_era(); arC.new_era()
        if stage not in ("full", "p2", "p2c"):
            break
        if stage != "full":
            break

    for s_ in all_store_sems:
        v_ = s_.n
        if v_:
            B.steps.append(("gpsimd", lambda e, s_=s_, v_=v_: e.wait_ge(s_.h, v_)))

    with nc.Block() as block:
        @block.sync
        def _(eng):
            for name, step in B.steps:
                if name == "sync":
                    step(eng)

        @block.tensor
        def _(eng):
            for name, step in B.steps:
                if name == "tensor":
                    step(eng)

        @block.vector
        def _(eng):
            for name, step in B.steps:
                if name == "vector":
                    step(eng)

        @block.scalar
        def _(eng):
            for name, step in B.steps:
                if name == "scalar":
                    step(eng)

        @block.gpsimd
        def _(eng):
            for name, step in B.steps:
                if name == "gpsimd":
                    step(eng)

    return nc, dbg_aps


def host_inputs(x, rkv_w, out_w, td, tf, tdr, tfr, wg, wu, wd):
    return {
        "x": np.ascontiguousarray(x.reshape(-1, C)),
        "rkvw_t": np.ascontiguousarray(rkv_w.T),
        "outw_t": np.ascontiguousarray(out_w.T),
        "wg": np.ascontiguousarray(wg),
        "wu": np.ascontiguousarray(wu),
        "wd": np.ascontiguousarray(wd),
        "tdec": np.ascontiguousarray(np.stack([td, tdr])),
        "tfir": np.ascontiguousarray(np.stack([tf, tfr])),
        "iden": np.eye(128, dtype=np.float32),
    }


from concourse.bass_utils import run_bass_kernel_spmd

_NCORES = 8
_T = 2048
_BLOC = 2


def kernel(L0_attn_x, L0_attn_rkv_w, L0_attn_out_w, L0_attn_time_decay,
           L0_attn_time_first, L0_attn_time_decay_rev, L0_attn_time_first_rev,
           L0_ffn_w_gate, L0_ffn_w_up, L0_ffn_w_down):
    x = np.ascontiguousarray(np.asarray(L0_attn_x, dtype=np.float32))
    Bfull = x.shape[0]
    shared = {
        "rkvw_t": np.ascontiguousarray(np.asarray(L0_attn_rkv_w, np.float32).T),
        "outw_t": np.ascontiguousarray(np.asarray(L0_attn_out_w, np.float32).T),
        "wg": np.ascontiguousarray(np.asarray(L0_ffn_w_gate, np.float32)),
        "wu": np.ascontiguousarray(np.asarray(L0_ffn_w_up, np.float32)),
        "wd": np.ascontiguousarray(np.asarray(L0_ffn_w_down, np.float32)),
        "tdec": np.ascontiguousarray(np.stack([np.asarray(L0_attn_time_decay, np.float32),
                                               np.asarray(L0_attn_time_decay_rev, np.float32)])),
        "tfir": np.ascontiguousarray(np.stack([np.asarray(L0_attn_time_first, np.float32),
                                               np.asarray(L0_attn_time_first_rev, np.float32)])),
        "iden": np.eye(128, dtype=np.float32),
    }
    nc = bass.Bass("TRN2", target_bir_lowering=False)
    build(nc, T=_T, BLOC=_BLOC, dbg=(), stage="full")
    in_maps = []
    for c in range(_NCORES):
        m = dict(shared)
        m["x"] = np.ascontiguousarray(
            x[c * _BLOC:(c + 1) * _BLOC].reshape(_BLOC * _T, C))
        in_maps.append(m)
    res = run_bass_kernel_spmd(nc, in_maps, core_ids=list(range(_NCORES)))
    out = np.empty((Bfull, _T, C), dtype=np.float32)
    for c in range(_NCORES):
        out[c * _BLOC:(c + 1) * _BLOC] = np.asarray(
            res.results[c]["out"], dtype=np.float32).reshape(_BLOC, _T, C)
    return out
